# version 02
# baseline (speedup 1.0000x reference)
import math
import numpy as np
import ml_dtypes
from contextlib import ExitStack

import concourse.bass as bass
import concourse.mybir as mybir
from concourse.bass_utils import run_bass_kernel_spmd

F32 = mybir.dt.float32
BF16 = mybir.dt.bfloat16
I32 = mybir.dt.int32
AF = mybir.ActivationFunctionType
ALU = mybir.AluOpType
AX = mybir.AxisListType
BF = ml_dtypes.bfloat16

P = 128
D = 2048
NCH = 16
TCORE = 1024
TBATCH = 4096
NTILE = 8
NQC = 8
M_STAB = 16.0
ISQ = 1.0 / math.sqrt(128.0)
LN_EPS = 1e-5
UPTO = 4
DBG = False
QO = 16 * 512

SL16 = [2.0 ** (-0.5 * (i + 1)) for i in range(16)]

_PROG_CACHE = {}


def build_program():
    if "p" in _PROG_CACHE:
        return _PROG_CACHE["p"]
    nc = bass.Bass("TRN2", target_bir_lowering=False, num_devices=8,
                   dynamic_dma_scratch_size=16384)
    top = ExitStack()
    sems, cnts = {}, {}

    def sem(name):
        if name not in sems:
            sems[name] = top.enter_context(nc.semaphore(name))
            cnts[name] = 0
        return sems[name]

    def inc(ins, name, n=1):
        ins.then_inc(sem(name), n)
        cnts[name] += n
        return cnts[name]

    def cnt(name):
        sem(name)
        return cnts[name]

    def wge(eng, name, val):
        if val > 0:
            eng.wait_ge(sem(name), val)

    t = nc.dram_tensor
    x_in = t("x_in", [TCORE, D], F32, kind="ExternalInput").ap()
    wq_a = t("wq_a", [D, 512], F32, kind="ExternalInput").ap()
    wk_a = t("wk_a", [D, 128], F32, kind="ExternalInput").ap()
    wv_a = t("wv_a", [D, 128], F32, kind="ExternalInput").ap()
    wo_a = t("wo_a", [D, D], F32, kind="ExternalInput").ap()
    wq_b = t("wq_b", [D, 512], F32, kind="ExternalInput").ap()
    wk_b = t("wk_b", [D, 256], F32, kind="ExternalInput").ap()
    wv_b = t("wv_b", [D, 256], F32, kind="ExternalInput").ap()
    wo_b = t("wo_b", [D, D], F32, kind="ExternalInput").ap()
    gw_t = t("gw_t", [D, 32], F32, kind="ExternalInput").ap()
    gu_in = t("gu_in", [32 * 1536, D], F32, kind="ExternalInput").ap()
    dn_in = t("dn_in", [32 * D, 768], F32, kind="ExternalInput").ap()
    mg_in = t("mg_in", [32, 1], I32, kind="ExternalInput").ap()
    dome_in = t("dome_in", [8, 1], I32, kind="ExternalInput").ap()
    cvec_in = t("cvec_in", [P, 1], I32, kind="ExternalInput").ap()
    ident_in = t("ident_in", [P, P], BF16, kind="ExternalInput").ap()
    identf_in = t("identf_in", [P, P], F32, kind="ExternalInput").ap()
    masks_in = t("masks_in", [P, 4 * 512], BF16, kind="ExternalInput").ap()
    alibi_in = t("alibi_in", [12, TBATCH], BF16, kind="ExternalInput").ap()
    oaidx_in = t("oaidx_in", [P, 32], I32, kind="ExternalInput").ap()
    cbias_in = t("cbias_in", [P, 2], F32, kind="ExternalInput").ap()
    agw_in = t("agw_in", [1, 2], I32, kind="ExternalInput").ap()
    x_out = t("x_out", [TCORE, D], F32, kind="ExternalOutput").ap()
    if DBG:
        dbg_lnall = t("dbg_lnall", [4 * TCORE, D], BF16,
                      kind="ExternalOutput").ap()
        dbg_oall = t("dbg_oall", [8 * D, 512], BF16,
                     kind="ExternalOutput").ap()
        dbg_qfm = t("dbg_qfm", [P, 4 * TBATCH], BF16,
                    kind="ExternalOutput").ap()
        dbg_kfm = t("dbg_kfm", [P, TBATCH], BF16,
                    kind="ExternalOutput").ap()
        dbg_vtm = t("dbg_vtm", [P, TBATCH], BF16,
                    kind="ExternalOutput").ap()
        dbg_oown = t("dbg_oown", [P, 32 * 512], BF16,
                     kind="ExternalOutput").ap()
        dbg_wosb = t("dbg_wosb", [P, 16 * 512], BF16,
                     kind="ExternalOutput").ap()

    ln_cd = [t(f"ln_c{i}", [TCORE, D], BF16).ap() for i in range(2)]
    ln_alld = [t(f"ln_all{i}", [4 * TCORE, D], BF16).ap() for i in range(2)]
    ln_tm = [t(f"ln_tm{i}", [TCORE, D], BF16).ap() for i in range(2)]
    o_cd = [t(f"o_c{i}", [8 * 512, 512], BF16).ap() for i in range(2)]
    o_alld = [t(f"o_all{i}", [8 * D, 512], BF16).ap() for i in range(2)]
    guc = t("guc", [D, 1536], BF16).ap()
    gu_all = t("gu_all", [8 * D, 1536], BF16).ap()
    dnc = t("dnc", [768, D], BF16).ap()
    dn_all = t("dn_all", [8 * 768, D], BF16).ap()

    def sbuf(st_, name, shape, dt, **kw):
        return st_.enter_context(nc.sbuf_tensor(name, shape, dt, **kw))

    x_sb = sbuf(top, "x_sb", [P, NTILE * D], F32)
    ident = sbuf(top, "ident", [P, P], BF16)
    identf = sbuf(top, "identf", [P, P], F32)
    ones_col = sbuf(top, "ones_col", [P, 1], BF16)
    ones_rf = sbuf(top, "ones_rf", [1, P], F32)
    cvec_sb = sbuf(top, "cvec_sb", [P, 1], I32)
    mg_sb = sbuf(top, "mg_sb", [32, 1], I32)
    mg_f = sbuf(top, "mg_f", [32, 1], F32)
    G32 = sbuf(top, "G32", [32, 8], BF16)
    dom_sb = sbuf(top, "dom_sb", [P, 1], I32)
    iota_p = sbuf(top, "iota_p", [P, 1], I32)
    idx_sb = sbuf(top, "idx_sb", [P, 1], I32)
    oaidx_sb = sbuf(top, "oaidx_sb", [P, 32], I32)
    gwsb = sbuf(top, "gwsb", [P, 16 * 32], BF16)
    probs = sbuf(top, "probs", [P, 32], F32)
    cmp_sc = sbuf(top, "cmp_sc", [P, 1024], BF16)
    rank_sb = sbuf(top, "rank_sb", [P, 32], F32)
    sel_sb = sbuf(top, "sel_sb", [P, 32], F32)
    mp_sb = sbuf(top, "mp_sb", [P, 32], BF16)
    mp_fm = sbuf(top, "mp_fm", [32, P], BF16)
    wraw = sbuf(top, "wraw", [P, 8], F32)
    wtg_all = sbuf(top, "wtg_all", [P, 8 * NTILE], F32)
    gs1 = sbuf(top, "gs1", [P, 1], F32)
    gs2 = sbuf(top, "gs2", [P, 1], F32)
    gs3 = sbuf(top, "gs3", [P, 1], F32)
    gs4 = sbuf(top, "gs4", [P, 1], F32)
    cbias = sbuf(top, "cbias", [P, 2], F32)
    agw_sb = sbuf(top, "agw_sb", [1, 2], I32)

    ps_o = [top.enter_context(nc.psum_tensor(f"ps_o{i}", [P, 512], F32))
            for i in range(4)]
    ps_s = [top.enter_context(nc.psum_tensor(f"ps_s{i}", [P, 512], F32))
            for i in range(2)]
    ps_r = top.enter_context(nc.psum_tensor("ps_r", [P, 512], F32))
    ps_t = top.enter_context(nc.psum_tensor("ps_t", [P, 1024], BF16))

    blk = top.enter_context(nc.Block())
    ev = {}
    ascr = {}

    def wev(eng, tag):
        s, v = ev[tag]
        wge(eng, s, v)

    def ph0_sync(sync):
        for tt in range(NTILE):
            inc(sync.dma_start(x_sb[:, tt * D:(tt + 1) * D],
                               x_in[tt * P:(tt + 1) * P, :]), "sx", 16)
        for dst, src in [(ident, ident_in), (identf, identf_in),
                         (cvec_sb, cvec_in), (mg_sb, mg_in),
                         (oaidx_sb, oaidx_in), (cbias, cbias_in),
                         (agw_sb, agw_in)]:
            inc(sync.dma_start(dst[:, :], src[:, :]), "sx", 16)
        ev["consts"] = ("sx", cnt("sx"))
    blk.sync(ph0_sync)
    for tt in range(NTILE):
        ev[("xload", tt)] = ev["consts"]

    def ph0_gp(gp):
        src = bass.AP(gw_t.tensor, 0, [[32, P], [32 * P, NCH], [1, 32]])
        inc(gp.dma_start(gwsb[:, :], src), "sgd", 16)
        ev["gw"] = ("sgd", cnt("sgd"))
    blk.gpsimd(ph0_gp)

    def ph0_vec(v):
        v.memset(ones_col[:, :], 1.0)
        inc(v.memset(ones_rf[:, :], 1.0), "sv")
        ev["ones"] = ("sv", cnt("sv"))
        wev(v, "consts")
        inc(v.tensor_copy(mg_f[:, :], mg_sb[:, :]), "sv")
        wge(v, "sv", cnt("sv"))
        ins = None
        for g in range(8):
            ins = v.tensor_single_scalar(G32[:, g:g + 1], mg_f[:, 0:1],
                                         float(g), ALU.is_equal)
        inc(ins, "sv")
        ev["g32"] = ("sv", cnt("sv"))
    blk.vector(ph0_vec)

    def ln_emit(li, x_ready, scratch, dst_tm, gate_first=None):
        xsq, xhat, sts, tev = scratch
        (s_s, s_q, s_m, s_v, s_r, s_n, s_t2) = sts
        a0, v0 = cnt("sa"), cnt("sv")
        wl0 = [cnt(f"swl{i}") for i in range(4)]

        def f_sc(sc):
            for tt in range(NTILE):
                s, vv = x_ready[tt]
                wge(sc, s, vv)
                if tt == 0:
                    wev(sc, "consts")
                if tt == 0 and gate_first is not None:
                    wge(sc, gate_first[0], gate_first[1])
                pp = tt % 2
                xt = x_sb[:, tt * D:(tt + 1) * D]
                inc(sc.activation(xsq[:, :], xt, AF.Square,
                                  accum_out=s_q[pp][:, 0:1]), "sa")
                wge(sc, "sv", v0 + 6 * tt + 5)
                inc(sc.activation(s_v[pp][:, 0:1], s_v[pp][:, 0:1], AF.Ln,
                                  bias=cbias[:, 0:1]), "sa")
                wge(sc, "sa", a0 + 3 * tt + 2)
                inc(sc.activation(s_r[pp][:, 0:1], s_v[pp][:, 0:1], AF.Exp,
                                  scale=-0.5), "sa")
        blk.scalar(f_sc)

        def f_v(v):
            for tt in range(NTILE):
                s, vv = x_ready[tt]
                wge(v, s, vv)
                pp = tt % 2
                xt = x_sb[:, tt * D:(tt + 1) * D]
                inc(v.reduce_sum(s_s[pp][:, 0:1], xt, axis=AX.X), "sv")
                wge(v, "sv", v0 + 6 * tt + 1)
                inc(v.tensor_scalar(s_m[pp][:, 0:1], s_s[pp][:, 0:1],
                                    1.0 / D, None, ALU.mult), "sv")
                wge(v, "sv", v0 + 6 * tt + 2)
                wge(v, "sa", a0 + 3 * tt + 1)
                inc(v.scalar_tensor_tensor(s_t2[pp][:, 0:1], s_m[pp][:, 0:1],
                                           1.0, s_m[pp][:, 0:1],
                                           ALU.mult, ALU.mult), "sv")
                inc(v.tensor_scalar(s_v[pp][:, 0:1], s_q[pp][:, 0:1],
                                    1.0 / D, None, ALU.mult), "sv")
                wge(v, "sv", v0 + 6 * tt + 4)
                inc(v.tensor_sub(s_v[pp][:, 0:1], s_v[pp][:, 0:1],
                                 s_t2[pp][:, 0:1]), "sv")
                wge(v, "sa", a0 + 3 * tt + 3)
                if tt >= 2:
                    p4 = (tt - 2) % 4
                    wge(v, f"swl{p4}", wl0[p4] + 16 * ((tt - 2) // 4 + 1))
                inc(v.tensor_scalar(xhat[pp][:, :], xt, s_m[pp][:, 0:1],
                                    s_r[pp][:, 0:1], ALU.subtract, ALU.mult),
                    "sv")
        blk.vector(f_v)

        def f_sy(sync):
            for tt in range(NTILE):
                wge(sync, "sv", v0 + 6 * tt + 6)
                inc(sync.dma_start(dst_tm[tt * P:(tt + 1) * P, :],
                                   xhat[tt % 2][:, :]), f"swl{tt % 4}", 16)
                if tt == 3:
                    ev[("lnw", li, 0)] = [(f"swl{i}", cnt(f"swl{i}"))
                                          for i in range(4)]
            ev[("lnw", li, 1)] = [(f"swl{i}", cnt(f"swl{i}"))
                                  for i in range(4)]
        blk.sync(f_sy)

    def alloc_ln_scratch(st_, tag):
        xsq = sbuf(st_, f"xsq{tag}", [P, D], F32)
        xhat = [sbuf(st_, f"xh{tag}{i}", [P, D], BF16) for i in range(2)]
        sts = tuple([sbuf(st_, f"s{tag}{k}_{i}", [P, 1], F32)
                     for i in range(2)] for k in range(7))
        tev = [sbuf(st_, f"tv{tag}{i}", [P, 1024], BF16) for i in range(4)]
        return (xsq, xhat, sts, tev)

    def ln_to_slab(fi, li, x_ready, scratch, dst_slab, gate_first=None):
        ln_emit(li, x_ready, scratch, ln_tm[fi], gate_first=gate_first)

        def f_tr(sync):
            for s, v in ev[("lnw", li, 1)]:
                wge(sync, s, v)
            for cc in range(NCH):
                srcap = bass.AP(ln_tm[fi].tensor, cc * P,
                                [[D, TCORE], [1, P]])
                inc(sync.dma_start(dst_slab[:, cc * TCORE:(cc + 1) * TCORE],
                                   srcap, transpose=True), "slt", 16)
            ev[("lnslab", fi)] = ("slt", cnt("slt"))
        blk.sync(f_tr)

    def load_wqkv(av, wqkv, nkv, wq, wk, wv, gate_w):
        KVW = 256 * nkv
        KVH = 128 * nkv

        def f_gp_w(gp):
            if gate_w is not None:
                wge(gp, gate_w[0], gate_w[1])
            for cc in range(NCH):
                inc(gp.dma_start(wqkv[:, cc * 512:(cc + 1) * 512],
                                 wq[cc * P:(cc + 1) * P, :]), "swq", 16)
            for cc in range(NCH):
                for kv in range(nkv):
                    inc(gp.dma_start(
                        wqkv[:, QO + cc * KVW + kv * P:QO + cc * KVW + (kv + 1) * P],
                        wk[cc * P:(cc + 1) * P, kv * P:(kv + 1) * P]), "swq", 16)
                    inc(gp.dma_start(
                        wqkv[:, QO + cc * KVW + KVH + kv * P:
                             QO + cc * KVW + KVH + (kv + 1) * P],
                        wv[cc * P:(cc + 1) * P, kv * P:(kv + 1) * P]), "swq", 16)
            ev[("wqkv", av)] = ("swq", cnt("swq"))
        blk.gpsimd(f_gp_w)

    def qkv_phase(av, wqkv, lslab, Q_fm, K_fm, V_tm, lnall, nkv,
                  gate_slab):
        KVW = 256 * nkv
        KVH = 128 * nkv

        t0, a0, e0 = cnt("stn"), cnt("sev"), cnt("sa")
        t0, a0, e0 = cnt("stn"), cnt("sa"), cnt("sev")
        TPC = 4 + 3 * nkv
        APC = 4 + 2 * nkv

        def f_sy(sync):
            for tc in range(NQC):
                h, r = tc // 4, tc % 4
                if tc == 0 and gate_slab is not None:
                    wge(sync, gate_slab[0], gate_slab[1])
                if tc == 0 or tc == 4:
                    wev(sync, ("ag_ln", av, h))
                if tc >= 2:
                    wge(sync, "stn", t0 + (tc - 1) * TPC)
                for cc in range(NCH):
                    src = bass.AP(lnall.tensor,
                                  (h * 2048 + r * 512) * D + cc * P,
                                  [[D, 512], [1, P]])
                    inc(sync.dma_start(lslab[tc % 2][:, cc * 512:(cc + 1) * 512],
                                       src, transpose=True),
                        f"sl{tc % 2}", 16)
                ev[("lsl", av, tc)] = (f"sl{tc % 2}", cnt(f"sl{tc % 2}"))
        blk.sync(f_sy)

        def f_t(tensor):
            for tc in range(NQC):
                s, vv = ev[("lsl", av, tc)]
                wge(tensor, s, vv)
                if tc == 0:
                    wev(tensor, ("wqkv", av))
                    wge(tensor, "sev", e0)
                    wge(tensor, "sv", cnt("sv"))
                ls = lslab[tc % 2]
                for dq in range(4):
                    if tc >= 1:
                        wge(tensor, "sa", a0 + (tc - 1) * APC + dq + 1)
                    ins = None
                    for cc in range(NCH):
                        ins = tensor.matmul(
                            ps_o[dq][:, :],
                            wqkv[:, cc * 512 + dq * P:cc * 512 + (dq + 1) * P],
                            ls[:, cc * 512:(cc + 1) * 512],
                            start=(cc == 0), stop=(cc == NCH - 1))
                    inc(ins, "stn")
                for kv in range(nkv):
                    if tc >= 1:
                        wge(tensor, "sa", a0 + (tc - 1) * APC + 4 + kv + 1)
                    ins = None
                    for cc in range(NCH):
                        ins = tensor.matmul(
                            ps_s[kv][:, :],
                            wqkv[:, QO + cc * KVW + kv * P:
                                 QO + cc * KVW + (kv + 1) * P],
                            ls[:, cc * 512:(cc + 1) * 512],
                            start=(cc == 0), stop=(cc == NCH - 1))
                    inc(ins, "stn")
                for kv in range(nkv):
                    wge(tensor, "sa", a0 + tc * APC + 4 + kv + 1)
                    ins = None
                    for cc in range(NCH):
                        ins = tensor.matmul(
                            ps_s[kv][:, :],
                            wqkv[:, QO + cc * KVW + KVH + kv * P:
                                 QO + cc * KVW + KVH + (kv + 1) * P],
                            ls[:, cc * 512:(cc + 1) * 512],
                            start=(cc == 0), stop=(cc == NCH - 1))
                    inc(ins, "stn")
                for kv in range(nkv):
                    wge(tensor, "sa", a0 + tc * APC + 4 + nkv + kv + 1)
                    gi = tc * nkv + kv
                    if gi >= 1:
                        wge(tensor, "sev", e0 + gi)
                    ins = None
                    for q in range(4):
                        ins = tensor.transpose(
                            ps_t[:, q * P:(q + 1) * P],
                            ascr['ost'][kv % 2][:, q * P:(q + 1) * P],
                            ident[:, :])
                    inc(ins, "stn")
        blk.tensor(f_t)

        def f_sc(sc):
            for tc in range(NQC):
                h, r = tc // 4, tc % 4
                gt0 = r * 1024 + h * 512
                for dq in range(4):
                    wge(sc, "stn", t0 + tc * TPC + dq + 1)
                    inc(sc.activation(Q_fm[:, dq * TBATCH + gt0:
                                           dq * TBATCH + gt0 + 512],
                                      ps_o[dq][:, :], AF.Copy), "sa")
                for kv in range(nkv):
                    wge(sc, "stn", t0 + tc * TPC + 4 + kv + 1)
                    inc(sc.activation(K_fm[:, kv * TBATCH + gt0:
                                           kv * TBATCH + gt0 + 512],
                                      ps_s[kv][:, :], AF.Copy, scale=ISQ),
                        "sa")
                for kv in range(nkv):
                    wge(sc, "stn", t0 + tc * TPC + 4 + nkv + kv + 1)
                    gi = tc * nkv + kv
                    gprev = gi - (1 if nkv == 1 else 2)
                    if gprev >= 0:
                        tcp, kvp = divmod(gprev, nkv)
                        wge(sc, "stn", t0 + tcp * TPC + 4 + 2 * nkv + kvp + 1)
                    inc(sc.activation(ascr['ost'][kv % 2][:, :], ps_s[kv][:, :],
                                      AF.Copy), "sa")
        blk.scalar(f_sc)

        def f_vev(v):
            for tc in range(NQC):
                h, r = tc // 4, tc % 4
                gt0 = r * 1024 + h * 512
                for kv in range(nkv):
                    wge(v, "stn", t0 + tc * TPC + 4 + 2 * nkv + kv + 1)
                    i_ap = bass.AP(ps_t, 0, [[1024, P], [P, 4], [1, P]])
                    o_ap = bass.AP(V_tm, kv * TBATCH + gt0 * P // 128 * 128,
                                   [[nkv * TBATCH, P], [P, 4], [1, P]])
                    o_ap = bass.AP(V_tm, kv * TBATCH + gt0,
                                   [[nkv * TBATCH, P], [P, 4], [1, P]])
                    inc(v.tensor_copy(o_ap, i_ap), "sev")
        blk.vector(f_vev)
        ev[("qkvdone", av)] = ("stn", cnt("stn"))

    def attn_phase(av, Q_fm, K_fm, V_tm, nkv, alibi_kq, o_cdram):
        H = 4 // nkv
        causal = (av == 0)
        swp0 = [cnt(f"swp{i}") for i in range(4)]
        for qc in range(NQC):
            w_qc0 = cnt("sw")
            for kv in range(nkv):
                kbc = 4 * qc + 4 if causal else 32
                N = kbc * H
                t0, a0, v0 = cnt("stn"), cnt("sa"), cnt("sv")
                pv0, prv0, w0 = cnt("spv"), cnt("sprv"), cnt("sw")
                nfo = []
                for n in range(N):
                    kb, qh = divmod(n, H)
                    nfo.append((kb, qh, causal and kb >= 4 * qc))
                cdiag = [0]
                for (_, _, d) in nfo:
                    cdiag.append(cdiag[-1] + (1 if d else 0))
                prevdiag = [None] * max(N, 1)
                lastd = None
                for n2 in range(N):
                    prevdiag[n2] = lastd
                    if nfo[n2][2]:
                        lastd = n2

                def f_t(tensor, qc=qc, kv=kv, N=N, nfo=nfo, t0=t0, a0=a0,
                        v0=v0, pv0=pv0, prv0=prv0, cdiag=cdiag, kbc=kbc):
                    def pv_of(m):
                        kb, qh, diag = nfo[m]
                        wge(tensor, "sa", a0 + m + 1)
                        oq = kv * H + qh
                        tensor.matmul(
                            ps_o[oq][:, :],
                            V_tm[:, kv * TBATCH + kb * P:kv * TBATCH + (kb + 1) * P],
                            ascr['ptil'][m % 4][:, :],
                            start=(kb == 0), stop=(kb == kbc - 1))
                        ins = tensor.matmul(
                            ps_r[32 * qh:32 * qh + 1, :],
                            ones_col[:, 0:1], ascr['ptil'][m % 4][:, :],
                            start=(kb == 0), stop=(kb == kbc - 1),
                            tile_position=(0, 32 * qh))
                        inc(ins, "spv")
                    for n in range(N):
                        kb, qh, diag = nfo[n]
                        if n < 2:
                            wge(tensor, "sa", a0)
                            wge(tensor, "sv", v0)
                            wge(tensor, "spv", pv0)
                            wge(tensor, "sev", cnt("sev"))
                            if qc == 0 and kv == 0 and n == 0:
                                wev(tensor, ("qkvdone", av))
                        else:
                            wge(tensor, "sa", a0 + n - 1)
                        bank = ps_s[n % 2]
                        qcol = (kv * H + qh) * TBATCH + qc * 512
                        ins = tensor.matmul(
                            bank[:, :],
                            K_fm[:, kv * TBATCH + kb * P:kv * TBATCH + (kb + 1) * P],
                            Q_fm[:, qcol:qcol + 512],
                            start=True, stop=not causal)
                        if causal:
                            ins = tensor.matmul(
                                bank[:, :],
                                alibi_kk[0:6, kb * P:(kb + 1) * P],
                                alibi_qq[0:6, qc * 512:(qc + 1) * 512],
                                start=False, stop=True)
                        inc(ins, "stn")
                        if n >= 2:
                            pv_of(n - 2)
                    if N >= 2:
                        pv_of(N - 2)
                    pv_of(N - 1)
                    for qh in range(H):
                        wge(tensor, "sv", v0 + 2 * qh + 1)
                        if qh >= 1:
                            wge(tensor, "sv", v0 + 2 * qh)
                        ins = tensor.matmul(ps_s[0][:, :], ones_rf[0:1, 0:P],
                                            ascr['r'][0:1, :], start=True, stop=True)
                        inc(ins, "stn")
                blk.tensor(f_t)

                def f_sc(sc, qc=qc, kv=kv, N=N, nfo=nfo, t0=t0, a0=a0, v0=v0,
                         pv0=pv0, prv0=prv0, cdiag=cdiag, kbc=kbc):
                    for n in range(N):
                        kb, qh, diag = nfo[n]
                        wge(sc, "stn", t0 + n + 1)
                        wge(sc, "spv", pv0 + max(0, n - 3))
                        bank = ps_s[n % 2]
                        if diag:
                            wge(sc, "sprv", prv0 + cdiag[n] + 1)
                            inc(sc.activation(ascr['ptil'][n % 4][:, :], ascr['sm'][:, :],
                                              AF.Exp, bias=cbias[:, 1:2]),
                                "sa")
                        else:
                            inc(sc.activation(ascr['ptil'][n % 4][:, :], bank[:, :],
                                              AF.Exp, bias=cbias[:, 1:2]), "sa")
                    for qh in range(H):
                        m_last = (kbc - 1) * H + qh
                        wge(sc, "spv", pv0 + m_last + 1)
                        wge(sc, "sv", v0 + 2 * qh)
                        inc(sc.activation(ascr['oev'][:, :], ps_o[kv * H + qh][:, :],
                                          AF.Copy), "sa")
                blk.scalar(f_sc)

                def f_v(v, qc=qc, kv=kv, N=N, nfo=nfo, t0=t0, a0=a0, v0=v0,
                        pv0=pv0, prv0=prv0, cdiag=cdiag, kbc=kbc, w0=w0,
                        swp0=swp0, prevdiag=prevdiag):
                    first_mask = True
                    for n in range(N):
                        kb, qh, diag = nfo[n]
                        if not diag:
                            continue
                        if first_mask and qc == 0 and kv == 0:
                            wev(v, "amask")
                        first_mask = False
                        wge(v, "stn", t0 + n + 1)
                        if cdiag[n] > 0:
                            wge(v, "sa", a0 + prevdiag[n] + 1)
                        di = kb - 4 * qc
                        inc(v.tensor_add(ascr['sm'][:, :], ps_s[n % 2][:, :],
                                         masks[:, di * 512:(di + 1) * 512]),
                            "sprv")
                    for qh in range(H):
                        m_last = (kbc - 1) * H + qh
                        wge(v, "spv", pv0 + m_last + 1)
                        wge(v, "stn", t0 + N + qh)
                        inc(v.reciprocal(ascr['r'][0:1, :],
                                         ps_r[32 * qh:32 * qh + 1, :]), "sv")
                        wge(v, "sa", a0 + N + qh + 1)
                        wge(v, "stn", t0 + N + qh + 1)
                        hl = kv * H + qh
                        hlp = hl - 2
                        if hlp >= 0:
                            wge(v, f"swp{hlp}", swp0[hlp] + 16 * (qc + 1))
                        elif qc >= 1:
                            wge(v, f"swp{hl + 2}", swp0[hl + 2] + 16 * qc)
                        inc(v.tensor_mul(ascr['ost'][qh % 2][:, :], ascr['oev'][:, :],
                                         ps_s[0][:, :]), "sv")
                blk.vector(f_v)

                def f_sy(sync, qc=qc, kv=kv, H=H, v0=v0, o_cdram=o_cdram):
                    for qh in range(H):
                        wge(sync, "sv", v0 + 2 * qh + 2)
                        hl = kv * H + qh
                        inc(sync.dma_start(
                            o_cdram[qc * 512 + hl * P:qc * 512 + (hl + 1) * P,
                                    :],
                            ascr['ost'][qh % 2][:, :]), f"swp{hl}", 16)
                blk.sync(f_sy)
            def f_gp(gp, qc=qc, swp0=swp0):
                for i in range(4):
                    wge(gp, f"swp{i}", swp0[i] + 16 * (qc + 1))
                inc(gp.collective_compute(
                    "AllGather", ALU.bypass,
                    replica_groups=[[0, 1, 2, 3], [4, 5, 6, 7]],
                    ins=[bass.AP(o_cdram.tensor, qc * 512 * 512,
                                 [[512, 512], [1, 512]])],
                    outs=[bass.AP(o_alld_cur.tensor, qc * D * 512,
                                  [[512, 4 * 512], [1, 512]])]), "scc")
                ev[("ag_o", av, qc)] = ("scc", cnt("scc"))
            o_alld_cur = o_alld[av]
            o_cdram = o_cd[av]
            blk.gpsimd(f_gp)

    def wo_phase(av, O_own, wo_sbuf, wo_w, lx):
        t0, e0 = cnt("stn"), cnt("sev")
        sv_entry = cnt("sv")
        gw0 = [cnt("sgw0"), cnt("sgw1")]

        def f_gp(gp):
            with gp.register(f"ragw{av}") as ragw:
                gp.reg_load(ragw, agw_sb[0:1, av:av + 1])
                gp.wait_ge(sem("scc"), ragw)
            g0o = cnt("sgo")
            for ti in range(32):
                if ti >= 4:
                    wge(gp, "sgo", g0o + 16 * (ti - 3))
                inc(gp.indirect_dma_start(
                    O_own[:, ti * 512:(ti + 1) * 512], None,
                    o_alld[av][:, :],
                    bass.IndirectOffsetOnAxis(oaidx_sb[:, ti:ti + 1], 0)),
                    "sgo", 16)
            ev[("oga", av)] = ("sgo", cnt("sgo"))
            for nd in range(4):
                if nd >= 2:
                    wge(gp, "stn", t0 + (nd - 1) * 8)
                src = bass.AP(wo_w.tensor, nd * 512,
                              [[D, P], [P * D, NCH], [1, 512]])
                inc(gp.dma_start(wo_sbuf[nd % 2][:, :], src),
                    f"sgw{nd % 2}", 16)
                ev[("wosl", av, nd)] = (f"sgw{nd % 2}",
                                        cnt(f"sgw{nd % 2}"))
        blk.gpsimd(f_gp)

        def f_t(tensor):
            wev(tensor, ("oga", av))
            wge(tensor, "sv", sv_entry)
            wge(tensor, "sev", e0)
            for nd in range(4):
                wev(tensor, ("wosl", av, nd))
                for tt in range(NTILE):
                    gi = nd * 8 + tt
                    if gi >= 2:
                        wge(tensor, "sev", e0 + gi - 1)
                    ins = None
                    for hc in range(NCH):
                        ins = tensor.matmul(
                            ps_s[gi % 2][:, :],
                            O_own[:, ((tt // 4) * 16 + hc) * 512 +
                                  (tt % 4) * P:
                                  ((tt // 4) * 16 + hc) * 512 + (tt % 4 + 1) * P],
                            wo_sbuf[nd % 2][:, hc * 512:(hc + 1) * 512],
                            start=(hc == 0), stop=(hc == NCH - 1))
                    inc(ins, "stn")
        blk.tensor(f_t)

        def f_v(v):
            for nd in range(4):
                for tt in range(NTILE):
                    gi = nd * 8 + tt
                    wge(v, "stn", t0 + gi + 1)
                    inc(v.scalar_tensor_tensor(
                        x_sb[:, tt * D + nd * 512:tt * D + (nd + 1) * 512],
                        ps_s[gi % 2][:, :], 1.0,
                        x_sb[:, tt * D + nd * 512:tt * D + (nd + 1) * 512],
                        ALU.mult, ALU.add), "sev")
                    if nd == 3:
                        ev[("xf", lx, tt)] = ("sev", cnt("sev"))
        blk.vector(f_v)
        ev[("wodone", av)] = ("stn", cnt("stn"))

    def gating_phase(fi, lnf):
        for tt in range(NTILE):
            def f_t1(tensor, tt=tt):
                if tt == 0:
                    wev(tensor, ("lnslab", fi))
                    wev(tensor, "g32")
                else:
                    wev(tensor, ("grk", fi, tt - 1))
                ins = None
                for cc in range(NCH):
                    ins = tensor.matmul(
                        ps_r[:, 0:32],
                        lnf[:, cc * TCORE + tt * P:cc * TCORE + tt * P + P],
                        gwsb[:, cc * 32:(cc + 1) * 32],
                        start=(cc == 0), stop=(cc == NCH - 1))
                inc(ins, "stn")
                ev[("glog", fi, tt)] = ("stn", cnt("stn"))
            blk.tensor(f_t1)

            def f_v1(v, tt=tt):
                wev(v, ("glog", fi, tt))
                inc(v.reduce_max(gs2[:, 0:1], ps_r[:, 0:32], axis=AX.X), "sv")
                wge(v, "sv", cnt("sv"))
                inc(v.tensor_scalar(gs3[:, 0:1], gs2[:, 0:1], -1.0, None,
                                    ALU.mult), "sv")
                ev[("gnm", fi, tt)] = ("sv", cnt("sv"))
            blk.vector(f_v1)

            def f_s1(sc, tt=tt):
                wev(sc, ("gnm", fi, tt))
                inc(sc.activation(probs[:, :], ps_r[:, 0:32], AF.Exp,
                                  bias=gs3[:, 0:1], accum_out=gs1[:, 0:1]),
                    "sa")
                ev[("gexp", fi, tt)] = ("sa", cnt("sa"))
            blk.scalar(f_s1)

            def f_v2(v, tt=tt):
                wev(v, ("gexp", fi, tt))
                inc(v.reciprocal(gs4[:, 0:1], gs1[:, 0:1]), "sv")
                wge(v, "sv", cnt("sv"))
                inc(v.tensor_scalar(probs[:, :], probs[:, :], gs4[:, 0:1],
                                    None, ALU.mult), "sv")
                wge(v, "sv", cnt("sv"))
                in_j = bass.AP(probs, 0, [[32, P], [1, 32], [0, 32]])
                in_e = bass.AP(probs, 0, [[32, P], [0, 32], [1, 32]])
                o_c3 = bass.AP(cmp_sc, 0, [[1024, P], [32, 32], [1, 32]])
                inc(v.tensor_tensor(o_c3, in_j, in_e, ALU.is_gt), "sv")
                wge(v, "sv", cnt("sv"))
                in_r = bass.AP(cmp_sc, 0, [[1024, P], [1, 32], [32, 32]])
                o_r = bass.AP(rank_sb, 0, [[32, P], [1, 32], [0, 1]])
                inc(v.reduce_sum(o_r, in_r, axis=AX.X), "sv")
                wge(v, "sv", cnt("sv"))
                inc(v.tensor_single_scalar(sel_sb[:, :], rank_sb[:, :], 8.0,
                                           ALU.is_lt), "sv")
                wge(v, "sv", cnt("sv"))
                inc(v.tensor_mul(mp_sb[:, :], probs[:, :], sel_sb[:, :]), "sv")
                ev[("gmp", fi, tt)] = ("sv", cnt("sv"))
            blk.vector(f_v2)

            def f_t2(tensor, tt=tt):
                wev(tensor, ("gmp", fi, tt))
                if tt > 0:
                    wev(tensor, ("gwr2", fi, tt - 1))
                ins = tensor.transpose(ps_t[0:32, 0:P], mp_sb[:, 0:32],
                                       ident[:, :])
                inc(ins, "stn")
                ev[("gmpt", fi, tt)] = ("stn", cnt("stn"))
            blk.tensor(f_t2)

            def f_v3(v, tt=tt):
                wev(v, ("gmpt", fi, tt))
                inc(v.tensor_copy(mp_fm[0:32, 0:P], ps_t[0:32, 0:P]), "sv")
                ev[("gmpf", fi, tt)] = ("sv", cnt("sv"))
            blk.vector(f_v3)

            def f_t3(tensor, tt=tt):
                wev(tensor, ("gmpf", fi, tt))
                wev(tensor, ("glog", fi, tt))
                wev(tensor, ("gexp", fi, tt))
                ins = tensor.matmul(ps_r[:, 0:8], mp_fm[0:32, 0:P],
                                    G32[0:32, 0:8], start=True, stop=True)
                inc(ins, "stn")
                ev[("gwrm", fi, tt)] = ("stn", cnt("stn"))
            blk.tensor(f_t3)

            def f_v4(v, tt=tt):
                wev(v, ("gwrm", fi, tt))
                inc(v.tensor_copy(wraw[:, :], ps_r[:, 0:8]), "sv")
                ev[("grk", fi, tt)] = ("sv", cnt("sv"))
                wge(v, "sv", cnt("sv"))
                inc(v.reduce_sum(gs1[:, 0:1], wraw[:, :], axis=AX.X), "sv")
                wge(v, "sv", cnt("sv"))
                inc(v.reciprocal(gs4[:, 0:1], gs1[:, 0:1]), "sv")
                wge(v, "sv", cnt("sv"))
                inc(v.tensor_scalar(wtg_all[:, tt * 8:(tt + 1) * 8],
                                    wraw[:, :], gs4[:, 0:1], None, ALU.mult),
                    "sv")
                ev[("gwr2", fi, tt)] = ("sv", cnt("sv"))
            blk.vector(f_v4)
        ev[("gdone", fi)] = ("sv", cnt("sv"))

    def ffn_phase(fi, lnf, act_fm, dslab, gu_tl, fsc, lx, gate_loads):
        w_half, wbc, upsc, silu1 = fsc
        for h in range(2):
            t0 = cnt("stn")
            def fmm_val(ld, t0=t0):
                return t0 + (ld // 12) * 14 + 2 + (ld % 12) + 1

            def f_sy_gu(sync, h=h, t0=t0):
                ld = 0
                for g in range(8):
                    for ic in range(6):
                        for up in range(2):
                            oc = ic + 6 * up
                            if ld == 0:
                                wev(sync, "ag_gu")
                                if gate_loads is not None:
                                    wge(sync, gate_loads[0], gate_loads[1])
                            if ld >= 2:
                                wge(sync, "stn", fmm_val(ld - 2))
                            if ld < 2 and h == 1:
                                wge(sync, "stn", t0)
                            src = bass.AP(gu_all.tensor,
                                          g * D * 1536 + oc * P,
                                          [[1536, P], [P * 1536, NCH],
                                           [1, P]])
                            inc(sync.dma_start(gu_tl[ld % 2][:, :], src),
                                f"sf{ld % 2}", 16)
                            ev[("gusl", fi, h, ld)] = (f"sf{ld % 2}",
                                                       cnt(f"sf{ld % 2}"))
                            ld += 1
            blk.sync(f_sy_gu)

            e0, a0, v0 = cnt("sev"), cnt("sa"), cnt("sv")

            def f_t_gu(tensor, h=h, t0=t0, e0=e0, a0=a0, v0=v0):
                ld = 0
                for g in range(8):
                    if g == 0:
                        wev(tensor, ("gwr2", fi, h * 4 + 3))
                    if h == 0 and g == 0:
                        wev(tensor, ("lnslab", fi))
                    wge(tensor, "sv", v0 + g * 3)
                    ins = None
                    for t2 in range(4):
                        col = (h * 4 + t2) * 8 + g
                        ins = tensor.transpose(
                            bass.AP(ps_r, t2 * P, [[512, 1], [1, P]]),
                            wtg_all[:, col:col + 1], identf[:, :])
                    inc(ins, "stn")
                    wge(tensor, "sv", v0 + g * 3 + 1)
                    ins = tensor.matmul(ps_o[1][:, :], ones_rf[0:1, 0:P],
                                        w_half[0:1, :], start=True, stop=True)
                    inc(ins, "stn")
                    for ic in range(6):
                        for up in range(2):
                            wev(tensor, ("gusl", fi, h, ld))
                            i_g = g * 6 + ic
                            if up == 0 and i_g >= 1:
                                wge(tensor, "sa", a0 + i_g)
                            if up == 1 and i_g >= 1:
                                wge(tensor, "sev", e0 + (i_g - 1) * 2 + 1)
                            ins = None
                            for cc in range(NCH):
                                ins = tensor.matmul(
                                    ps_s[up][:, :],
                                    gu_tl[ld % 2][:, cc * P:(cc + 1) * P],
                                    lnf[:, cc * TCORE + h * 512:
                                        cc * TCORE + (h + 1) * 512],
                                    start=(cc == 0), stop=(cc == NCH - 1))
                            inc(ins, "stn")
                            ld += 1
            blk.tensor(f_t_gu)



            def f_s_gu(sc, h=h, t0=t0, e0=e0):
                for g in range(8):
                    for ic in range(6):
                        i_g = g * 6 + ic
                        wge(sc, "stn", t0 + g * 14 + 2 + 2 * ic + 1)
                        if i_g >= 1:
                            wge(sc, "sev", e0 + (i_g - 1) * 2 + 2)
                        inc(sc.activation(silu1[:, :], ps_s[0][:, :],
                                          AF.Silu), "sa")
            blk.scalar(f_s_gu)

            def f_v_gu(v, h=h, t0=t0, a0=a0, v0=v0, e0=e0):
                for g in range(8):
                    wge(v, "stn", t0 + g * 14 + 1)
                    inc(v.tensor_copy(w_half[0:1, :],
                                      bass.AP(ps_r, 0,
                                              [[512, 1], [1, 512]])), "sv")
                    wge(v, "stn", t0 + g * 14 + 2)
                    inc(v.tensor_copy(wbc[:, :], ps_o[1][:, :]), "sv")
                    inc(v.engine_nop(), "sv")
                    wge(v, "sv", v0 + g * 3 + 2)
                    for ic in range(6):
                        i_g = g * 6 + ic
                        wge(v, "stn", t0 + g * 14 + 2 + 2 * ic + 2)
                        inc(v.tensor_mul(upsc[:, :], ps_s[1][:, :],
                                         wbc[:, :]), "sev")
                        wge(v, "sev", e0 + i_g * 2 + 1)
                        wge(v, "sa", a0 + i_g + 1)
                        inc(v.tensor_mul(
                            act_fm[:, (g * 6 + ic) * 512:
                                   (g * 6 + ic + 1) * 512],
                            silu1[:, :], upsc[:, :]), "sev")
            blk.vector(f_v_gu)

            t0d = cnt("stn") + 16 * 14 - t0 + t0
            t0d = t0 + 8 * 14
            acts_done = e0 + 96

            def f_sy_dn(sync, h=h, t0d=t0d, fi=fi):
                for hc in range(8):
                    if hc >= 2:
                        wge(sync, "stn", t0d + (hc - 1) * 4)
                    if hc < 2:
                        wge(sync, "stn", t0d)
                        wev(sync, "ag_dn")
                    if hc < 2 and gate_loads is not None:
                        wge(sync, gate_loads[0], gate_loads[1])
                    src = bass.AP(dn_all.tensor, hc * 256 + h * 0,
                                  [[D, P], [P * D, 48], [1, 256]])
                    inc(sync.dma_start(dslab[hc % 2][:, :], src),
                        f"sd{hc % 2}", 16)
                    ev[("dnsl", fi, h, hc)] = (f"sd{hc % 2}",
                                               cnt(f"sd{hc % 2}"))
            blk.sync(f_sy_dn)

            e0d = cnt("sev")

            def f_t_dn(tensor, h=h, t0d=t0d, e0d=e0d, acts_done=acts_done):
                for hc in range(8):
                    wev(tensor, ("dnsl", fi, h, hc))
                    if hc == 0:
                        wge(tensor, "sev", acts_done)
                    for t2 in range(4):
                        gi = hc * 4 + t2
                        if gi >= 2:
                            wge(tensor, "sev", e0d + gi - 1)
                        ins = None
                        for g in range(8):
                            for ic in range(6):
                                k = g * 6 + ic
                                ins = tensor.matmul(
                                    bass.AP(ps_s[gi % 2], 0,
                                            [[512, P], [1, 256]]),
                                    act_fm[:, k * 512 + t2 * P:
                                           k * 512 + (t2 + 1) * P],
                                    dslab[hc % 2][:, k * 256:(k + 1) * 256],
                                    start=(k == 0), stop=(k == 47))
                        inc(ins, "stn")
            blk.tensor(f_t_dn)

            def f_v_dn(v, h=h, t0d=t0d, lx=lx):
                for hc in range(8):
                    for t2 in range(4):
                        gi = hc * 4 + t2
                        wge(v, "stn", t0d + gi + 1)
                        tt = h * 4 + t2
                        xo = x_sb[:, tt * D + hc * 256:
                                  tt * D + (hc + 1) * 256]
                        inc(v.scalar_tensor_tensor(
                            xo, bass.AP(ps_s[gi % 2], 0,
                                        [[512, P], [1, 256]]),
                            1.0, xo, ALU.mult, ALU.add), "sev")
                        if hc == 7:
                            ev[("xf", lx, tt)] = ("sev", cnt("sev"))
            blk.vector(f_v_dn)
        ev[("ffndone", fi)] = ("stn", cnt("stn"))

    sW0 = ExitStack()
    wqkvA = sbuf(sW0, "wqkvA", [P, QO + 16 * 256], BF16)
    sGth = ExitStack()
    gath32 = [sbuf(sGth, f"gath32_{i}", [P, D], F32) for i in range(2)]
    gath16 = [sbuf(sGth, f"gath16_{i}", [P, D], BF16) for i in range(2)]
    idx2 = [sbuf(sGth, f"idx2_{i}", [P, 1], I32) for i in range(2)]

    lnS = ExitStack()
    scr = alloc_ln_scratch(lnS, "A")
    ln_emit(0, [ev[("xload", tt)] for tt in range(NTILE)], scr, ln_cd[0])
    lnS.close()

    def mk_gp_ag_ln(li, av):
        def gp_ag_ln(gp):
            for h in range(2):
                for s, v in ev[("lnw", li, h)]:
                    wge(gp, s, v)
                inc(gp.collective_compute(
                    "AllGather", ALU.bypass,
                    replica_groups=[[0, 1, 2, 3], [4, 5, 6, 7]],
                    ins=[bass.AP(ln_cd[av].tensor, h * 512 * D,
                                 [[D, 512], [1, D]])],
                    outs=[bass.AP(ln_alld[av].tensor, h * 2048 * D,
                                  [[D, 2048], [1, D]])]), "scc")
                ev[("ag_ln", av, h)] = ("scc", cnt("scc"))
        return gp_ag_ln

    load_wqkv(0, wqkvA, 1, wq_a, wk_a, wv_a, None)
    blk.gpsimd(mk_gp_ag_ln(0, 0))

    def gp_dom(gp):
        wev(gp, "consts")
        inc(gp.indirect_dma_start(
            dom_sb[:, 0:1], None, dome_in[:, :],
            bass.IndirectOffsetOnAxis(cvec_sb[:, 0:1], 0)), "sgd", 16)
        ev["dom"] = ("sgd", cnt("sgd"))
        inc(gp.iota(iota_p[:, 0:1], [[0, 1]], channel_multiplier=1), "sio", 1)
        ev["iota"] = ("sio", cnt("sio"))
    blk.gpsimd(gp_dom)

    chunks = [("gu", i) for i in range(12)] + [("dn", i) for i in range(16)]
    v0g, t0g, e0g, w0g = (cnt("sv"), cnt("stn"), cnt("sev"), cnt("swc"))
    gg0 = [cnt("sgg0"), cnt("sgg1")]
    ngroups = []
    for i, (kind, ccc) in enumerate(chunks):
        nblk = 16 if kind == "gu" else 6
        for g in range((nblk + 7) // 8):
            ngroups.append((i, g, kind, ccc, min(8, nblk - g * 8)))
    cumg = [0]
    for kind, ccc in chunks:
        cumg.append(cumg[-1] + (2 if kind == "gu" else 1))
    grp_of_chunk = [[] for _ in chunks]
    for n, (i, g, kind, ccc, qn) in enumerate(ngroups):
        grp_of_chunk[i].append((n, qn))

    def gev(v, n, qn):
        wge(v, "stn", t0g + n + 1)
        if n >= 1:
            wge(v, "swc", w0g + n * 16)
        i_ap = bass.AP(ps_t, 0, [[1024, P], [P, qn], [1, P]])
        o_ap = bass.AP(cmp_sc, 0, [[1024, P], [P, qn], [1, P]])
        inc(v.tensor_copy(o_ap, i_ap), "sev")

    def f_v_idx(v):
        for i, (kind, ccc) in enumerate(chunks):
            if i == 0:
                wev(v, "iota")
                wev(v, "dom")
            rows = 1536 if kind == "gu" else D
            wid = D if kind == "gu" else 768
            if i >= 2:
                wge(v, f"sgg{i % 2}", gg0[i % 2] + 16 * (i // 2))
            inc(v.tensor_scalar(idx2[i % 2][:, 0:1], dom_sb[:, 0:1], rows,
                                ccc * P, ALU.mult, ALU.add), "sv")
            wge(v, "sv", v0g + 3 * i + 1)
            inc(v.tensor_add(idx2[i % 2][:, 0:1], idx2[i % 2][:, 0:1],
                             iota_p[:, 0:1]), "sv")
            wge(v, f"sgg{i % 2}",
                gg0[i % 2] + 16 * (i // 2 + 1))
            if i >= 1:
                for n, qn in grp_of_chunk[i - 1]:
                    gev(v, n, qn)
                wge(v, "stn", t0g + cumg[i])
            inc(v.tensor_copy(gath16[i % 2][:, 0:wid],
                              gath32[i % 2][:, 0:wid]), "sv")
        for n, qn in grp_of_chunk[len(chunks) - 1]:
            gev(v, n, qn)
        ev["gath_done"] = ("sev", cnt("sev"))
    blk.vector(f_v_idx)

    def gout(gp, n, qn, kind, ccc, g):
        wge(gp, "sev", e0g + n + 1)
        dstd = guc if kind == "gu" else dnc
        wid = 1536 if kind == "gu" else D
        d_ap = bass.AP(dstd.tensor, (g * 8) * P * wid + ccc * P,
                       [[wid, P], [P * wid, qn], [1, P]])
        s_ap = bass.AP(cmp_sc, 0, [[1024, P], [P, qn], [1, P]])
        inc(gp.dma_start(d_ap, s_ap), "swc", 16)

    def f_gp_gather(gp):
        for i, (kind, ccc) in enumerate(chunks):
            wid = D if kind == "gu" else 768
            srcd = gu_in if kind == "gu" else dn_in
            wge(gp, "sv", v0g + 3 * i + 2)
            if i >= 2:
                wge(gp, "sv", v0g + 3 * (i - 1))
            inc(gp.indirect_dma_start(
                gath32[i % 2][:, 0:wid], None, srcd[:, :],
                bass.IndirectOffsetOnAxis(idx2[i % 2][:, 0:1], 0)),
                f"sgg{i % 2}", 16)
            if i >= 1:
                kp, cp = chunks[i - 1]
                for (n, qn) in grp_of_chunk[i - 1]:
                    gg = n - grp_of_chunk[i - 1][0][0]
                    gout(gp, n, qn, kp, cp, gg)
        kp, cp = chunks[-1]
        for (n, qn) in grp_of_chunk[len(chunks) - 1]:
            gg = n - grp_of_chunk[len(chunks) - 1][0][0]
            gout(gp, n, qn, kp, cp, gg)
        ev["gwr"] = ("swc", cnt("swc"))
    blk.gpsimd(f_gp_gather)

    def f_t_g(tensor):
        for n, (i, g, kind, ccc, qn) in enumerate(ngroups):
            wge(tensor, "sv", v0g + 3 * i + 3)
            if n >= 1:
                wge(tensor, "sev", e0g + n)
            ins = None
            for q in range(qn):
                cc = g * 8 + q
                ins = tensor.transpose(ps_t[:, q * P:(q + 1) * P],
                                       gath16[i % 2][:, cc * P:(cc + 1) * P],
                                       ident[:, :])
            inc(ins, "stn")
    blk.tensor(f_t_g)


    def gp_ag_w(gp):
        wev(gp, "gwr")
        inc(gp.collective_compute(
            "AllGather", ALU.bypass, replica_groups=[list(range(8))],
            ins=[guc.opt()], outs=[gu_all.opt()]), "scc")
        ev["ag_gu"] = ("scc", cnt("scc"))
        inc(gp.collective_compute(
            "AllGather", ALU.bypass, replica_groups=[list(range(8))],
            ins=[dnc.opt()], outs=[dn_all.opt()]), "scc")
        ev["ag_dn"] = ("scc", cnt("scc"))
    blk.gpsimd(gp_ag_w)

    sGth.close()

    sA = ExitStack()
    alibi_kk = sbuf(sA, "alibi_kk", [6, TBATCH], BF16)
    alibi_qq = sbuf(sA, "alibi_qq", [6, TBATCH], BF16)
    masks = sbuf(sA, "masks", [P, 4 * 512], BF16)


    ascr['ptil'] = [sbuf(sA, f"ptilA{i}", [P, 512], BF16) for i in range(4)]
    ascr['sm'] = sbuf(sA, "smA", [P, 512], F32)
    ascr['oev'] = sbuf(sA, "oevA", [P, 512], BF16)
    ascr['r'] = sbuf(sA, "rA", [1, 512], F32)
    ascr['ost'] = [sbuf(sA, f"ostA{i}", [P, 512], BF16) for i in range(2)]
    Q_fmA = sbuf(sA, "Q_fmA", [P, 4 * TBATCH], BF16)
    K_fmA = sbuf(sA, "K_fmA", [P, TBATCH], BF16)
    V_tmA = sbuf(sA, "V_tmA", [P, TBATCH], BF16)
    sA3 = ExitStack()
    lslabA = [sbuf(sA3, f"lslabA{i}", [P, 16 * 512], BF16) for i in range(2)]
    qkv_phase(0, wqkvA, lslabA, Q_fmA, K_fmA, V_tmA, ln_alld[0], 1, None)
    sA3.close()
    def ld_amask(sync):
        wev(sync, "gath_done")
        inc(sync.dma_start(masks[:, :], masks_in[:, :]), "sxm", 16)
        inc(sync.dma_start(alibi_kk[0:6, :], alibi_in[0:6, :]), "sxm", 16)
        inc(sync.dma_start(alibi_qq[0:6, :], alibi_in[6:12, :]), "sxm", 16)
        ev["amask"] = ("sxm", cnt("sxm"))
    blk.sync(ld_amask)

    def gate_attn_amask(tensor):
        wev(tensor, "amask")
    blk.tensor(gate_attn_amask)

    attn_phase(0, Q_fmA, K_fmA, V_tmA, 1, None, o_cd[0])
    _SW0_CLOSE_MARK = True
    if DBG:
        def f_dbgq(sync):
            s, v = ev[("qkvdone", 0)]
            wge(sync, s, v)
            inc(sync.dma_start(dbg_qfm[:, :], Q_fmA[:, :]), "sw", 16)
            inc(sync.dma_start(dbg_kfm[:, :], K_fmA[:, :]), "sw", 16)
            inc(sync.dma_start(dbg_vtm[:, :], V_tmA[:, :]), "sw", 16)
        blk.sync(f_dbgq)
    sA.close()

    sW = ExitStack()
    O_ownA = sbuf(sW, "O_ownA", [P, 32 * 512], BF16)
    wo_sbA = [sbuf(sW, f"wo_sbA{i}", [P, 16 * 512], BF16) for i in range(2)]
    wo_phase(0, O_ownA, wo_sbA, wo_a, 1)
    if DBG:
        def f_dbgo(sync):
            wge(sync, "stn", cnt("stn"))
            inc(sync.dma_start(dbg_oown[:, :], O_ownA[:, :]), "sw", 16)
            inc(sync.dma_start(dbg_wosb[:, :], wo_sbA[1][:, :]), "sw", 16)
        blk.sync(f_dbgo)
    sW.close()
    sW0.close()

    if UPTO >= 2:
        sF = ExitStack()
        ln2_fm = sbuf(sF, "ln2_fm", [P, NCH * TCORE], BF16)
        sL2 = ExitStack()
        scr2 = alloc_ln_scratch(sL2, "B")
        ln_to_slab(0, 10, [ev[("xf", 1, tt)] for tt in range(NTILE)],
                   scr2, ln2_fm, gate_first=ev[("wodone", 0)])
        sL2.close()
        gating_phase(0, ln2_fm)
        sFC = ExitStack()
        act_fm1 = sbuf(sFC, "act_fm1", [P, 48 * 512], BF16)
        sh48a = sbuf(sFC, "sh48a", [P, 2 * 48 * 256], BF16)
        dslab1 = [sh48a[:, i * 48 * 256:(i + 1) * 48 * 256] for i in range(2)]
        gu_tl1 = [sh48a[:, i * 2048:(i + 1) * 2048] for i in range(2)]
        fsc1 = (sbuf(sFC, "w_half1", [1, 512], F32),
                sbuf(sFC, "wbc1", [P, 512], F32),
                sbuf(sFC, "upsc1", [P, 512], BF16),
                sbuf(sFC, "silu1a", [P, 512], BF16))
        ffn_phase(0, ln2_fm, act_fm1, dslab1, gu_tl1, fsc1, 2,
                  ev[("wodone", 0)])
        sFC.close()
        sF.close()

    if UPTO >= 3:
        sL3 = ExitStack()
        scr3 = alloc_ln_scratch(sL3, "C")
        ln_emit(1, [ev[("xf", 2, tt)] for tt in range(NTILE)], scr3,
                ln_cd[1], gate_first=ev[("ffndone", 0)])
        sL3.close()
        blk.gpsimd(mk_gp_ag_ln(1, 1))

        sB = ExitStack()
        ascr['ptil'] = [sbuf(sB, f"ptilB{i}", [P, 512], BF16)
                        for i in range(4)]
        ascr['oev'] = sbuf(sB, "oevB", [P, 512], BF16)
        ascr['r'] = sbuf(sB, "rB", [1, 512], F32)
        ascr['ost'] = [sbuf(sB, f"ostB{i}", [P, 512], BF16) for i in range(2)]
        Q_fmB = sbuf(sB, "Q_fmB", [P, 4 * TBATCH], BF16)
        K_fmB = sbuf(sB, "K_fmB", [P, 2 * TBATCH], BF16)
        V_tmB = sbuf(sB, "V_tmB", [P, 2 * TBATCH], BF16)
        sB3 = ExitStack()
        wqkvB = sbuf(sB3, "wqkvB", [P, QO + 16 * 512], BF16)
        lslabB = [sbuf(sB3, f"lslabB{i}", [P, 16 * 512], BF16)
                  for i in range(2)]
        load_wqkv(1, wqkvB, 2, wq_b, wk_b, wv_b, ev[("ffndone", 0)])
        qkv_phase(1, wqkvB, lslabB, Q_fmB, K_fmB, V_tmB, ln_alld[1], 2,
                  ev[("ffndone", 0)])
        sB3.close()
        attn_phase(1, Q_fmB, K_fmB, V_tmB, 2, None, o_cd[1])
        sB.close()

        sW2 = ExitStack()
        O_ownB = sbuf(sW2, "O_ownB", [P, 32 * 512], BF16)
        wo_sbB = [sbuf(sW2, f"wo_sbB{i}", [P, 16 * 512], BF16)
                  for i in range(2)]
        wo_phase(1, O_ownB, wo_sbB, wo_b, 3)
        sW2.close()

    if UPTO >= 4:
        sF2 = ExitStack()
        ln4_fm = sbuf(sF2, "ln4_fm", [P, NCH * TCORE], BF16)
        sL4 = ExitStack()
        scr4 = alloc_ln_scratch(sL4, "Dd")
        ln_to_slab(1, 11, [ev[("xf", 3, tt)] for tt in range(NTILE)],
                   scr4, ln4_fm, gate_first=ev[("wodone", 1)])
        sL4.close()
        gating_phase(1, ln4_fm)
        sFC2 = ExitStack()
        act_fm2 = sbuf(sFC2, "act_fm2", [P, 48 * 512], BF16)
        sh48b = sbuf(sFC2, "sh48b", [P, 2 * 48 * 256], BF16)
        dslab2 = [sh48b[:, i * 48 * 256:(i + 1) * 48 * 256] for i in range(2)]
        gu_tl2 = [sh48b[:, i * 2048:(i + 1) * 2048] for i in range(2)]
        fsc2 = (sbuf(sFC2, "w_half2", [1, 512], F32),
                sbuf(sFC2, "wbc2", [P, 512], F32),
                sbuf(sFC2, "upsc2", [P, 512], BF16),
                sbuf(sFC2, "silu2a", [P, 512], BF16))
        ffn_phase(1, ln4_fm, act_fm2, dslab2, gu_tl2, fsc2, 4,
                  ev[("wodone", 1)])
        sFC2.close()
        sF2.close()

    out_lx = UPTO

    def f_out(sync):
        for tt in range(NTILE):
            s, vv = ev[("xf", out_lx, tt)]
            wge(sync, s, vv)
            inc(sync.dma_start(x_out[tt * P:(tt + 1) * P, :],
                               x_sb[:, tt * D:(tt + 1) * D]), "sw", 16)
        if DBG:
            inc(sync.dma_start(dbg_lnall[:, :], ln_alld[0][:, :]), "sw", 16)
            inc(sync.dma_start(dbg_oall[:, :], o_alld[0][:, :]), "sw", 16)
        wge(sync, "sw", cnt("sw"))
    blk.sync(f_out)

    top.close()
    _PROG_CACHE["p"] = nc
    return nc


def _split3(v):
    v = np.asarray(v, np.float32)
    hi = v.astype(BF)
    mid = (v - hi.astype(np.float32)).astype(BF)
    lo = (v - hi.astype(np.float32) - mid.astype(np.float32)).astype(BF)
    return hi, mid, lo


def _host_inputs(inputs, c):
    b, j = c // 4, c % 4
    x = inputs["L0_attn_a_x"]
    Wq, Wk, Wv, Wo = (inputs["L0_attn_a_Wq"], inputs["L0_attn_a_Wk"],
                      inputs["L0_attn_a_Wv"], inputs["L0_attn_a_Wo"])
    wqb, wkb, wvb, wob = (inputs["L0_attn_b_wq"], inputs["L0_attn_b_wk"],
                          inputs["L0_attn_b_wv"], inputs["L0_attn_b_wo"])
    gwe = inputs["L0_ffn_gate_weight"]
    gup = inputs["L0_ffn_gate_up_proj"]
    dwn = inputs["L0_ffn_down_proj"]
    mg = np.asarray(inputs["L0_ffn_merge_groups"], np.int32).reshape(32, 1)
    dome = np.asarray(inputs["L0_ffn_dominant_experts"],
                      np.int32).reshape(8, 1)
    C = np.ascontiguousarray
    d = {
        "x_in": C(x[b, j * 1024:(j + 1) * 1024, :]),
        "wq_a": C(Wq[:, 4 * j * 128:(4 * j + 4) * 128]),
        "wk_a": C(Wk[:, j * 128:(j + 1) * 128]),
        "wv_a": C(Wv[:, j * 128:(j + 1) * 128]),
        "wo_a": C(Wo),
        "wq_b": C(wqb[4 * j * 128:(4 * j + 4) * 128, :].T),
        "wk_b": C(wkb[2 * j * 128:(2 * j + 2) * 128, :].T),
        "wv_b": C(wvb[2 * j * 128:(2 * j + 2) * 128, :].T),
        "wo_b": C(wob.T),
        "gw_t": C(gwe.T),
        "gu_in": C(gup.reshape(32 * 1536, D)),
        "dn_in": C(dwn.reshape(32 * D, 768)),
        "mg_in": mg,
        "dome_in": dome,
        "cvec_in": np.full((P, 1), c, np.int32),
        "ident_in": np.eye(P, dtype=np.float32).astype(BF),
        "identf_in": np.eye(P, dtype=np.float32),
        "cbias_in": np.tile(np.array([[LN_EPS, -M_STAB]], np.float32),
                            (P, 1)),
        "agw_in": np.array([[6 + 2 * j, 16 + 2 * j]], np.int32),
    }
    m = np.zeros((P, 4 * 512), np.float32)
    ql = np.arange(512)
    for di in range(4):
        kl = np.arange(P)[:, None] + di * 128
        m[:, di * 512:(di + 1) * 512] = np.where(kl <= ql[None, :],
                                                 0.0, -1e30)
    d["masks_in"] = m.astype(BF)
    slope = SL16[j]
    al = np.zeros((12, TBATCH), np.float32)
    kq = np.arange(TBATCH, dtype=np.float64)
    h, mi, lo = _split3(slope * kq)
    al[0, :] = h.astype(np.float32)
    al[1, :] = mi.astype(np.float32)
    al[2, :] = lo.astype(np.float32)
    al[3:6, :] = 1.0
    al[6:9, :] = 1.0
    h, mi, lo = _split3(-slope * kq)
    al[9, :] = h.astype(np.float32)
    al[10, :] = mi.astype(np.float32)
    al[11, :] = lo.astype(np.float32)
    d["alibi_in"] = al.astype(BF)
    oi = np.zeros((P, 32), np.int32)
    pcol = np.arange(P, dtype=np.int32)
    ti = 0
    for k in range(2):
        qc = 2 * j + k
        for r in range(4):
            for hl in range(4):
                oi[:, ti] = qc * D + r * 512 + hl * 128 + pcol
                ti += 1
    d["oaidx_in"] = oi
    return d


def kernel(**inputs):
    nc = build_program()
    in_maps = [_host_inputs(inputs, c) for c in range(8)]
    res = run_bass_kernel_spmd(nc, in_maps, core_ids=list(range(8)))
    x = inputs["L0_attn_a_x"]
    out = np.zeros((2, TBATCH, D), np.float32)
    for c in range(8):
        b, j = c // 4, c % 4
        out[b, j * 1024:(j + 1) * 1024, :] = res.results[c]["x_out"]
    return out


if __name__ == "__main__":
    build_program()
    print("build ok")


# version 04
# speedup vs baseline: 1.0251x; 1.0251x over previous
import math
import numpy as np
import ml_dtypes
from contextlib import ExitStack

import concourse.bass as bass
import concourse.mybir as mybir
from concourse.bass_utils import run_bass_kernel_spmd

F32 = mybir.dt.float32
BF16 = mybir.dt.bfloat16
I32 = mybir.dt.int32
AF = mybir.ActivationFunctionType
ALU = mybir.AluOpType
AX = mybir.AxisListType
BF = ml_dtypes.bfloat16

P = 128
D = 2048
NCH = 16
TCORE = 1024
TBATCH = 4096
NTILE = 8
NQC = 8
M_STAB = 16.0
ISQ = 1.0 / math.sqrt(128.0)
LN_EPS = 1e-5
UPTO = 4
DBG = False
QO = 16 * 512

SL16 = [2.0 ** (-0.5 * (i + 1)) for i in range(16)]

_PROG_CACHE = {}


def build_program():
    if "p" in _PROG_CACHE:
        return _PROG_CACHE["p"]
    nc = bass.Bass("TRN2", target_bir_lowering=False, num_devices=8,
                   dynamic_dma_scratch_size=16384)
    top = ExitStack()
    sems, cnts = {}, {}

    def sem(name):
        if name not in sems:
            sems[name] = top.enter_context(nc.semaphore(name))
            cnts[name] = 0
        return sems[name]

    def inc(ins, name, n=1):
        ins.then_inc(sem(name), n)
        cnts[name] += n
        return cnts[name]

    def cnt(name):
        sem(name)
        return cnts[name]

    def wge(eng, name, val):
        if val > 0:
            eng.wait_ge(sem(name), val)

    t = nc.dram_tensor
    x_in = t("x_in", [TCORE, D], F32, kind="ExternalInput").ap()
    wq_a = t("wq_a", [D, 512], F32, kind="ExternalInput").ap()
    wk_a = t("wk_a", [D, 128], F32, kind="ExternalInput").ap()
    wv_a = t("wv_a", [D, 128], F32, kind="ExternalInput").ap()
    wo_a = t("wo_a", [D, D], F32, kind="ExternalInput").ap()
    wq_b = t("wq_b", [D, 512], F32, kind="ExternalInput").ap()
    wk_b = t("wk_b", [D, 256], F32, kind="ExternalInput").ap()
    wv_b = t("wv_b", [D, 256], F32, kind="ExternalInput").ap()
    wo_b = t("wo_b", [D, D], F32, kind="ExternalInput").ap()
    gw_t = t("gw_t", [D, 32], F32, kind="ExternalInput").ap()
    gu_in = t("gu_in", [32 * 1536, D], F32, kind="ExternalInput").ap()
    dn_in = t("dn_in", [32 * D, 768], F32, kind="ExternalInput").ap()
    mg_in = t("mg_in", [32, 1], I32, kind="ExternalInput").ap()
    dome_in = t("dome_in", [8, 1], I32, kind="ExternalInput").ap()
    cvec_in = t("cvec_in", [P, 1], I32, kind="ExternalInput").ap()
    ident_in = t("ident_in", [P, P], BF16, kind="ExternalInput").ap()
    identf_in = t("identf_in", [P, P], F32, kind="ExternalInput").ap()
    masks_in = t("masks_in", [P, 4 * 512], BF16, kind="ExternalInput").ap()
    alibi_in = t("alibi_in", [12, TBATCH], BF16, kind="ExternalInput").ap()
    oaidx_in = t("oaidx_in", [P, 32], I32, kind="ExternalInput").ap()
    cbias_in = t("cbias_in", [P, 2], F32, kind="ExternalInput").ap()
    agw_in = t("agw_in", [1, 2], I32, kind="ExternalInput").ap()
    x_out = t("x_out", [TCORE, D], F32, kind="ExternalOutput").ap()
    if DBG:
        dbg_lnall = t("dbg_lnall", [4 * TCORE, D], BF16,
                      kind="ExternalOutput").ap()
        dbg_oall = t("dbg_oall", [8 * D, 512], BF16,
                     kind="ExternalOutput").ap()
        dbg_qfm = t("dbg_qfm", [P, 4 * TBATCH], BF16,
                    kind="ExternalOutput").ap()
        dbg_kfm = t("dbg_kfm", [P, TBATCH], BF16,
                    kind="ExternalOutput").ap()
        dbg_vtm = t("dbg_vtm", [P, TBATCH], BF16,
                    kind="ExternalOutput").ap()
        dbg_oown = t("dbg_oown", [P, 32 * 512], BF16,
                     kind="ExternalOutput").ap()
        dbg_wosb = t("dbg_wosb", [P, 16 * 512], BF16,
                     kind="ExternalOutput").ap()

    ln_cd = [t(f"ln_c{i}", [TCORE, D], BF16).ap() for i in range(2)]
    ln_alld = [t(f"ln_all{i}", [4 * TCORE, D], BF16).ap() for i in range(2)]
    ln_tm = [t(f"ln_tm{i}", [TCORE, D], BF16).ap() for i in range(2)]
    o_cd = [t(f"o_c{i}", [8 * 512, 512], BF16).ap() for i in range(2)]
    o_alld = [t(f"o_all{i}", [8 * D, 512], BF16).ap() for i in range(2)]
    guc = t("guc", [D, 1536], BF16).ap()
    gu_all = t("gu_all", [8 * D, 1536], BF16).ap()
    dnc = t("dnc", [768, D], BF16).ap()
    dn_all = t("dn_all", [8 * 768, D], BF16).ap()

    def sbuf(st_, name, shape, dt, **kw):
        return st_.enter_context(nc.sbuf_tensor(name, shape, dt, **kw))

    x_sb = sbuf(top, "x_sb", [P, NTILE * D], F32)
    ident = sbuf(top, "ident", [P, P], BF16)
    identf = sbuf(top, "identf", [P, P], F32)
    ones_col = sbuf(top, "ones_col", [P, 1], BF16)
    ones_rf = sbuf(top, "ones_rf", [1, P], F32)
    cvec_sb = sbuf(top, "cvec_sb", [P, 1], I32)
    mg_sb = sbuf(top, "mg_sb", [32, 1], I32)
    mg_f = sbuf(top, "mg_f", [32, 1], F32)
    G32 = sbuf(top, "G32", [32, 8], BF16)
    dom_sb = sbuf(top, "dom_sb", [P, 1], I32)
    iota_p = sbuf(top, "iota_p", [P, 1], I32)
    idx_sb = sbuf(top, "idx_sb", [P, 1], I32)
    oaidx_sb = sbuf(top, "oaidx_sb", [P, 32], I32)
    gwsb = sbuf(top, "gwsb", [P, 16 * 32], BF16)
    probs = sbuf(top, "probs", [P, 32], F32)
    cmp_sc = sbuf(top, "cmp_sc", [P, 1024], BF16)
    rank_sb = sbuf(top, "rank_sb", [P, 32], F32)
    sel_sb = sbuf(top, "sel_sb", [P, 32], F32)
    mp_sb = sbuf(top, "mp_sb", [P, 32], BF16)
    mp_fm = sbuf(top, "mp_fm", [32, P], BF16)
    wraw = sbuf(top, "wraw", [P, 8], F32)
    wtg_all = sbuf(top, "wtg_all", [P, 8 * NTILE], F32)
    gs1 = sbuf(top, "gs1", [P, 1], F32)
    gs2 = sbuf(top, "gs2", [P, 1], F32)
    gs3 = sbuf(top, "gs3", [P, 1], F32)
    gs4 = sbuf(top, "gs4", [P, 1], F32)
    cbias = sbuf(top, "cbias", [P, 2], F32)
    agw_sb = sbuf(top, "agw_sb", [1, 2], I32)

    ps_o = [top.enter_context(nc.psum_tensor(f"ps_o{i}", [P, 512], F32))
            for i in range(4)]
    ps_s = [top.enter_context(nc.psum_tensor(f"ps_s{i}", [P, 512], F32))
            for i in range(2)]
    ps_r = top.enter_context(nc.psum_tensor("ps_r", [P, 512], F32))
    ps_t = top.enter_context(nc.psum_tensor("ps_t", [P, 1024], BF16))

    blk = top.enter_context(nc.Block())
    ev = {}
    ascr = {}

    def wev(eng, tag):
        s, v = ev[tag]
        wge(eng, s, v)

    def ph0_sync(sync):
        for tt in range(NTILE):
            inc(sync.dma_start(x_sb[:, tt * D:(tt + 1) * D],
                               x_in[tt * P:(tt + 1) * P, :]), "sx", 16)
        for dst, src in [(ident, ident_in), (identf, identf_in),
                         (cvec_sb, cvec_in), (mg_sb, mg_in),
                         (oaidx_sb, oaidx_in), (cbias, cbias_in),
                         (agw_sb, agw_in)]:
            inc(sync.dma_start(dst[:, :], src[:, :]), "sx", 16)
        ev["consts"] = ("sx", cnt("sx"))
    blk.sync(ph0_sync)
    for tt in range(NTILE):
        ev[("xload", tt)] = ev["consts"]

    def ph0_gp(gp):
        src = bass.AP(gw_t.tensor, 0, [[32, P], [32 * P, NCH], [1, 32]])
        inc(gp.dma_start(gwsb[:, :], src), "sgd", 16)
        ev["gw"] = ("sgd", cnt("sgd"))
    blk.gpsimd(ph0_gp)

    def ph0_vec(v):
        v.memset(ones_col[:, :], 1.0)
        inc(v.memset(ones_rf[:, :], 1.0), "sv")
        ev["ones"] = ("sv", cnt("sv"))
        wev(v, "consts")
        inc(v.tensor_copy(mg_f[:, :], mg_sb[:, :]), "sv")
        wge(v, "sv", cnt("sv"))
        ins = None
        for g in range(8):
            ins = v.tensor_single_scalar(G32[:, g:g + 1], mg_f[:, 0:1],
                                         float(g), ALU.is_equal)
        inc(ins, "sv")
        ev["g32"] = ("sv", cnt("sv"))
    blk.vector(ph0_vec)

    def ln_emit(li, x_ready, scratch, dst_tm, gate_first=None):
        xsq, xhat, sts, tev = scratch
        (s_s, s_q, s_m, s_v, s_r, s_n, s_t2) = sts
        a0, v0 = cnt("sa"), cnt("sv")
        wl0 = [cnt(f"swl{i}") for i in range(4)]

        def f_sc(sc):
            for tt in range(NTILE):
                s, vv = x_ready[tt]
                wge(sc, s, vv)
                if tt == 0:
                    wev(sc, "consts")
                if tt == 0 and gate_first is not None:
                    wge(sc, gate_first[0], gate_first[1])
                pp = tt % 2
                xt = x_sb[:, tt * D:(tt + 1) * D]
                inc(sc.activation(xsq[:, :], xt, AF.Square,
                                  accum_out=s_q[pp][:, 0:1]), "sa")
                wge(sc, "sv", v0 + 6 * tt + 5)
                inc(sc.activation(s_v[pp][:, 0:1], s_v[pp][:, 0:1], AF.Ln,
                                  bias=cbias[:, 0:1]), "sa")
                wge(sc, "sa", a0 + 3 * tt + 2)
                inc(sc.activation(s_r[pp][:, 0:1], s_v[pp][:, 0:1], AF.Exp,
                                  scale=-0.5), "sa")
        blk.scalar(f_sc)

        def f_v(v):
            for tt in range(NTILE):
                s, vv = x_ready[tt]
                wge(v, s, vv)
                pp = tt % 2
                xt = x_sb[:, tt * D:(tt + 1) * D]
                inc(v.reduce_sum(s_s[pp][:, 0:1], xt, axis=AX.X), "sv")
                wge(v, "sv", v0 + 6 * tt + 1)
                inc(v.tensor_scalar(s_m[pp][:, 0:1], s_s[pp][:, 0:1],
                                    1.0 / D, None, ALU.mult), "sv")
                wge(v, "sv", v0 + 6 * tt + 2)
                wge(v, "sa", a0 + 3 * tt + 1)
                inc(v.scalar_tensor_tensor(s_t2[pp][:, 0:1], s_m[pp][:, 0:1],
                                           1.0, s_m[pp][:, 0:1],
                                           ALU.mult, ALU.mult), "sv")
                inc(v.tensor_scalar(s_v[pp][:, 0:1], s_q[pp][:, 0:1],
                                    1.0 / D, None, ALU.mult), "sv")
                wge(v, "sv", v0 + 6 * tt + 4)
                inc(v.tensor_sub(s_v[pp][:, 0:1], s_v[pp][:, 0:1],
                                 s_t2[pp][:, 0:1]), "sv")
                wge(v, "sa", a0 + 3 * tt + 3)
                if tt >= 2:
                    p4 = (tt - 2) % 4
                    wge(v, f"swl{p4}", wl0[p4] + 16 * ((tt - 2) // 4 + 1))
                inc(v.tensor_scalar(xhat[pp][:, :], xt, s_m[pp][:, 0:1],
                                    s_r[pp][:, 0:1], ALU.subtract, ALU.mult),
                    "sv")
        blk.vector(f_v)

        def f_sy(sync):
            for tt in range(NTILE):
                wge(sync, "sv", v0 + 6 * tt + 6)
                inc(sync.dma_start(dst_tm[tt * P:(tt + 1) * P, :],
                                   xhat[tt % 2][:, :]), f"swl{tt % 4}", 16)
                if tt == 3:
                    ev[("lnw", li, 0)] = [(f"swl{i}", cnt(f"swl{i}"))
                                          for i in range(4)]
            ev[("lnw", li, 1)] = [(f"swl{i}", cnt(f"swl{i}"))
                                  for i in range(4)]
        blk.sync(f_sy)

    def alloc_ln_scratch(st_, tag):
        xsq = sbuf(st_, f"xsq{tag}", [P, D], F32)
        xhat = [sbuf(st_, f"xh{tag}{i}", [P, D], BF16) for i in range(2)]
        sts = tuple([sbuf(st_, f"s{tag}{k}_{i}", [P, 1], F32)
                     for i in range(2)] for k in range(7))
        tev = [sbuf(st_, f"tv{tag}{i}", [P, 1024], BF16) for i in range(4)]
        return (xsq, xhat, sts, tev)

    def ln_to_slab(fi, li, x_ready, scratch, dst_slab, gate_first=None):
        ln_emit(li, x_ready, scratch, ln_tm[fi], gate_first=gate_first)

        def f_tr(sync):
            for s, v in ev[("lnw", li, 1)]:
                wge(sync, s, v)
            for cc in range(NCH):
                srcap = bass.AP(ln_tm[fi].tensor, cc * P,
                                [[D, TCORE], [1, P]])
                inc(sync.dma_start(dst_slab[:, cc * TCORE:(cc + 1) * TCORE],
                                   srcap, transpose=True), "slt", 16)
            ev[("lnslab", fi)] = ("slt", cnt("slt"))
        blk.sync(f_tr)

    def load_wqkv(av, wqkv, nkv, wq, wk, wv, gate_w):
        KVW = 256 * nkv
        KVH = 128 * nkv

        def f_gp_w(gp):
            if gate_w is not None:
                wge(gp, gate_w[0], gate_w[1])
            for cc in range(NCH):
                inc(gp.dma_start(wqkv[:, cc * 512:(cc + 1) * 512],
                                 wq[cc * P:(cc + 1) * P, :]), "swq", 16)
            for cc in range(NCH):
                for kv in range(nkv):
                    inc(gp.dma_start(
                        wqkv[:, QO + cc * KVW + kv * P:QO + cc * KVW + (kv + 1) * P],
                        wk[cc * P:(cc + 1) * P, kv * P:(kv + 1) * P]), "swq", 16)
                    inc(gp.dma_start(
                        wqkv[:, QO + cc * KVW + KVH + kv * P:
                             QO + cc * KVW + KVH + (kv + 1) * P],
                        wv[cc * P:(cc + 1) * P, kv * P:(kv + 1) * P]), "swq", 16)
            ev[("wqkv", av)] = ("swq", cnt("swq"))
        blk.gpsimd(f_gp_w)

    def qkv_phase(av, wqkv, lslab, Q_fm, K_fm, V_tm, lnall, nkv,
                  gate_slab):
        KVW = 256 * nkv
        KVH = 128 * nkv

        t0, a0, e0 = cnt("stn"), cnt("sev"), cnt("sa")
        t0, a0, e0 = cnt("stn"), cnt("sa"), cnt("sev")
        TPC = 4 + 3 * nkv
        APC = 4 + 2 * nkv

        def f_sy(sync):
            for tc in range(NQC):
                h, r = tc // 4, tc % 4
                if tc == 0 and gate_slab is not None:
                    wge(sync, gate_slab[0], gate_slab[1])
                if tc == 0 or tc == 4:
                    wev(sync, ("ag_ln", av, h))
                if tc >= 2:
                    wge(sync, "stn", t0 + (tc - 1) * TPC)
                for cc in range(NCH):
                    src = bass.AP(lnall.tensor,
                                  (h * 2048 + r * 512) * D + cc * P,
                                  [[D, 512], [1, P]])
                    inc(sync.dma_start(lslab[tc % 2][:, cc * 512:(cc + 1) * 512],
                                       src, transpose=True),
                        f"sl{tc % 2}", 16)
                ev[("lsl", av, tc)] = (f"sl{tc % 2}", cnt(f"sl{tc % 2}"))
        blk.sync(f_sy)

        def f_t(tensor):
            for tc in range(NQC):
                s, vv = ev[("lsl", av, tc)]
                wge(tensor, s, vv)
                if tc == 0:
                    wev(tensor, ("wqkv", av))
                    wge(tensor, "sev", e0)
                    wge(tensor, "sv", cnt("sv"))
                ls = lslab[tc % 2]
                for dq in range(4):
                    if tc >= 1:
                        wge(tensor, "sa", a0 + (tc - 1) * APC + dq + 1)
                    ins = None
                    for cc in range(NCH):
                        ins = tensor.matmul(
                            ps_o[dq][:, :],
                            wqkv[:, cc * 512 + dq * P:cc * 512 + (dq + 1) * P],
                            ls[:, cc * 512:(cc + 1) * 512],
                            start=(cc == 0), stop=(cc == NCH - 1))
                    inc(ins, "stn")
                for kv in range(nkv):
                    if tc >= 1:
                        wge(tensor, "sa", a0 + (tc - 1) * APC + 4 + kv + 1)
                    ins = None
                    for cc in range(NCH):
                        ins = tensor.matmul(
                            ps_s[kv][:, :],
                            wqkv[:, QO + cc * KVW + kv * P:
                                 QO + cc * KVW + (kv + 1) * P],
                            ls[:, cc * 512:(cc + 1) * 512],
                            start=(cc == 0), stop=(cc == NCH - 1))
                    inc(ins, "stn")
                for kv in range(nkv):
                    wge(tensor, "sa", a0 + tc * APC + 4 + kv + 1)
                    ins = None
                    for cc in range(NCH):
                        ins = tensor.matmul(
                            ps_s[kv][:, :],
                            wqkv[:, QO + cc * KVW + KVH + kv * P:
                                 QO + cc * KVW + KVH + (kv + 1) * P],
                            ls[:, cc * 512:(cc + 1) * 512],
                            start=(cc == 0), stop=(cc == NCH - 1))
                    inc(ins, "stn")
                for kv in range(nkv):
                    wge(tensor, "sa", a0 + tc * APC + 4 + nkv + kv + 1)
                    gi = tc * nkv + kv
                    if gi >= 1:
                        wge(tensor, "sev", e0 + gi)
                    ins = None
                    for q in range(4):
                        ins = tensor.transpose(
                            ps_t[:, q * P:(q + 1) * P],
                            ascr['ost'][kv % 2][:, q * P:(q + 1) * P],
                            ident[:, :])
                    inc(ins, "stn")
        blk.tensor(f_t)

        def f_sc(sc):
            for tc in range(NQC):
                h, r = tc // 4, tc % 4
                gt0 = r * 1024 + h * 512
                for dq in range(4):
                    wge(sc, "stn", t0 + tc * TPC + dq + 1)
                    inc(sc.activation(Q_fm[:, dq * TBATCH + gt0:
                                           dq * TBATCH + gt0 + 512],
                                      ps_o[dq][:, :], AF.Copy), "sa")
                for kv in range(nkv):
                    wge(sc, "stn", t0 + tc * TPC + 4 + kv + 1)
                    inc(sc.activation(K_fm[:, kv * TBATCH + gt0:
                                           kv * TBATCH + gt0 + 512],
                                      ps_s[kv][:, :], AF.Copy, scale=ISQ),
                        "sa")
                for kv in range(nkv):
                    wge(sc, "stn", t0 + tc * TPC + 4 + nkv + kv + 1)
                    gi = tc * nkv + kv
                    gprev = gi - (1 if nkv == 1 else 2)
                    if gprev >= 0:
                        tcp, kvp = divmod(gprev, nkv)
                        wge(sc, "stn", t0 + tcp * TPC + 4 + 2 * nkv + kvp + 1)
                    inc(sc.activation(ascr['ost'][kv % 2][:, :], ps_s[kv][:, :],
                                      AF.Copy), "sa")
        blk.scalar(f_sc)

        def f_vev(v):
            for tc in range(NQC):
                h, r = tc // 4, tc % 4
                gt0 = r * 1024 + h * 512
                for kv in range(nkv):
                    wge(v, "stn", t0 + tc * TPC + 4 + 2 * nkv + kv + 1)
                    i_ap = bass.AP(ps_t, 0, [[1024, P], [P, 4], [1, P]])
                    o_ap = bass.AP(V_tm, kv * TBATCH + gt0 * P // 128 * 128,
                                   [[nkv * TBATCH, P], [P, 4], [1, P]])
                    o_ap = bass.AP(V_tm, kv * TBATCH + gt0,
                                   [[nkv * TBATCH, P], [P, 4], [1, P]])
                    inc(v.tensor_copy(o_ap, i_ap), "sev")
        blk.vector(f_vev)
        ev[("qkvdone", av)] = ("stn", cnt("stn"))

    def attn_phase(av, Q_fm, K_fm, V_tm, nkv, alibi_kq, o_cdram):
        H = 4 // nkv
        causal = (av == 0)
        swp0 = [cnt(f"swp{i}") for i in range(4)]
        for qc in range(NQC):
            w_qc0 = cnt("sw")
            for kv in range(nkv):
                kbc = 4 * qc + 4 if causal else 32
                N = kbc * H
                t0, a0, v0 = cnt("stn"), cnt("sa"), cnt("sv")
                pv0, prv0, w0 = cnt("spv"), cnt("sprv"), cnt("sw")
                nfo = []
                for n in range(N):
                    kb, qh = divmod(n, H)
                    nfo.append((kb, qh, causal and kb >= 4 * qc))
                cdiag = [0]
                for (_, _, d) in nfo:
                    cdiag.append(cdiag[-1] + (1 if d else 0))
                prevdiag = [None] * max(N, 1)
                lastd = None
                for n2 in range(N):
                    prevdiag[n2] = lastd
                    if nfo[n2][2]:
                        lastd = n2

                def f_t(tensor, qc=qc, kv=kv, N=N, nfo=nfo, t0=t0, a0=a0,
                        v0=v0, pv0=pv0, prv0=prv0, cdiag=cdiag, kbc=kbc):
                    PB = 4 // H

                    def pvbatch(kb):
                        wge(tensor, "sa", a0 + (kb + 1) * H)
                        for qh in range(H):
                            oq = kv * H + qh
                            bi = (kb % PB) * H + qh
                            tensor.matmul(
                                ps_o[oq][:, :],
                                V_tm[:, kv * TBATCH + kb * P:
                                     kv * TBATCH + (kb + 1) * P],
                                ascr['ptil'][bi][:, :],
                                start=(kb == 0), stop=(kb == kbc - 1))
                        ins = None
                        for qh in range(H):
                            bi = (kb % PB) * H + qh
                            ins = tensor.matmul(
                                ps_r[32 * qh:32 * qh + 1, :],
                                ones_col[:, 0:1], ascr['ptil'][bi][:, :],
                                start=(kb == 0), stop=(kb == kbc - 1),
                                tile_position=(0, 32 * qh))
                        inc(ins, "spv")
                    for n in range(N):
                        kb, qh, diag = nfo[n]
                        if n < 2:
                            wge(tensor, "sa", a0)
                            wge(tensor, "sv", v0)
                            wge(tensor, "spv", pv0)
                            wge(tensor, "sev", cnt("sev"))
                            if qc == 0 and kv == 0 and n == 0:
                                wev(tensor, ("qkvdone", av))
                        else:
                            wge(tensor, "sa", a0 + n - 1)
                        bank = ps_s[n % 2]
                        qcol = (kv * H + qh) * TBATCH + qc * 512
                        ins = tensor.matmul(
                            bank[:, :],
                            K_fm[:, kv * TBATCH + kb * P:kv * TBATCH + (kb + 1) * P],
                            Q_fm[:, qcol:qcol + 512],
                            start=True, stop=not causal)
                        if causal:
                            ins = tensor.matmul(
                                bank[:, :],
                                alibi_kk[0:6, kb * P:(kb + 1) * P],
                                alibi_qq[0:6, qc * 512:(qc + 1) * 512],
                                start=False, stop=True)
                        inc(ins, "stn")
                        if qh == 1 and kb >= 1:
                            pvbatch(kb - 1)
                    pvbatch(kbc - 1)
                    for qh in range(H):
                        wge(tensor, "sv", v0 + 2 * qh + 1)
                        if qh >= 1:
                            wge(tensor, "sv", v0 + 2 * qh)
                        ins = tensor.matmul(ps_s[0][:, :], ones_rf[0:1, 0:P],
                                            ascr['r'][0:1, :], start=True, stop=True)
                        inc(ins, "stn")
                blk.tensor(f_t)

                def f_sc(sc, qc=qc, kv=kv, N=N, nfo=nfo, t0=t0, a0=a0, v0=v0,
                         pv0=pv0, prv0=prv0, cdiag=cdiag, kbc=kbc):
                    PB = 4 // H
                    for n in range(N):
                        kb, qh, diag = nfo[n]
                        bi = (kb % PB) * H + qh
                        wge(sc, "stn", t0 + n + 1)
                        wge(sc, "spv", pv0 + max(0, kb - PB + 1))
                        bank = ps_s[n % 2]
                        if diag:
                            wge(sc, "sprv", prv0 + cdiag[n] + 1)
                            inc(sc.activation(ascr['ptil'][bi][:, :], ascr['sm'][:, :],
                                              AF.Exp, bias=cbias[:, 1:2]),
                                "sa")
                        else:
                            inc(sc.activation(ascr['ptil'][bi][:, :], bank[:, :],
                                              AF.Exp, bias=cbias[:, 1:2]), "sa")
                    for qh in range(H):
                        wge(sc, "spv", pv0 + kbc)
                        wge(sc, "sv", v0 + 2 * qh)
                        inc(sc.activation(ascr['oev'][:, :], ps_o[kv * H + qh][:, :],
                                          AF.Copy), "sa")
                blk.scalar(f_sc)

                def f_v(v, qc=qc, kv=kv, N=N, nfo=nfo, t0=t0, a0=a0, v0=v0,
                        pv0=pv0, prv0=prv0, cdiag=cdiag, kbc=kbc, w0=w0,
                        swp0=swp0, prevdiag=prevdiag):
                    first_mask = True
                    for n in range(N):
                        kb, qh, diag = nfo[n]
                        if not diag:
                            continue
                        if first_mask and qc == 0 and kv == 0:
                            wev(v, "amask")
                        first_mask = False
                        wge(v, "stn", t0 + n + 1)
                        if cdiag[n] > 0:
                            wge(v, "sa", a0 + prevdiag[n] + 1)
                        di = kb - 4 * qc
                        inc(v.tensor_add(ascr['sm'][:, :], ps_s[n % 2][:, :],
                                         masks[:, di * 512:(di + 1) * 512]),
                            "sprv")
                    for qh in range(H):
                        wge(v, "spv", pv0 + kbc)
                        wge(v, "stn", t0 + N + qh)
                        inc(v.reciprocal(ascr['r'][0:1, :],
                                         ps_r[32 * qh:32 * qh + 1, :]), "sv")
                        wge(v, "sa", a0 + N + qh + 1)
                        wge(v, "stn", t0 + N + qh + 1)
                        hl = kv * H + qh
                        hlp = hl - 2
                        if hlp >= 0:
                            wge(v, f"swp{hlp}", swp0[hlp] + 16 * (qc + 1))
                        elif qc >= 1:
                            wge(v, f"swp{hl + 2}", swp0[hl + 2] + 16 * qc)
                        inc(v.tensor_mul(ascr['ost'][qh % 2][:, :], ascr['oev'][:, :],
                                         ps_s[0][:, :]), "sv")
                blk.vector(f_v)

                def f_sy(sync, qc=qc, kv=kv, H=H, v0=v0, o_cdram=o_cdram):
                    for qh in range(H):
                        wge(sync, "sv", v0 + 2 * qh + 2)
                        hl = kv * H + qh
                        inc(sync.dma_start(
                            o_cdram[qc * 512 + hl * P:qc * 512 + (hl + 1) * P,
                                    :],
                            ascr['ost'][qh % 2][:, :]), f"swp{hl}", 16)
                blk.sync(f_sy)
            def f_gp(gp, qc=qc, swp0=swp0):
                for i in range(4):
                    wge(gp, f"swp{i}", swp0[i] + 16 * (qc + 1))
                inc(gp.collective_compute(
                    "AllGather", ALU.bypass,
                    replica_groups=[[0, 1, 2, 3], [4, 5, 6, 7]],
                    ins=[bass.AP(o_cdram.tensor, qc * 512 * 512,
                                 [[512, 512], [1, 512]])],
                    outs=[bass.AP(o_alld_cur.tensor, qc * D * 512,
                                  [[512, 4 * 512], [1, 512]])]), "scc")
                ev[("ag_o", av, qc)] = ("scc", cnt("scc"))
            o_alld_cur = o_alld[av]
            o_cdram = o_cd[av]
            blk.gpsimd(f_gp)

    def wo_phase(av, O_own, wo_sbuf, wo_w, lx):
        t0, e0 = cnt("stn"), cnt("sev")
        sv_entry = cnt("sv")
        gw0 = [cnt("sgw0"), cnt("sgw1")]

        def f_gp(gp):
            with gp.register(f"ragw{av}") as ragw:
                gp.reg_load(ragw, agw_sb[0:1, av:av + 1])
                gp.wait_ge(sem("scc"), ragw)
            g0o = cnt("sgo")
            for ti in range(32):
                if ti >= 4:
                    wge(gp, "sgo", g0o + 16 * (ti - 3))
                inc(gp.indirect_dma_start(
                    O_own[:, ti * 512:(ti + 1) * 512], None,
                    o_alld[av][:, :],
                    bass.IndirectOffsetOnAxis(oaidx_sb[:, ti:ti + 1], 0)),
                    "sgo", 16)
            ev[("oga", av)] = ("sgo", cnt("sgo"))
            for nd in range(4):
                if nd >= 2:
                    wge(gp, "stn", t0 + (nd - 1) * 8)
                src = bass.AP(wo_w.tensor, nd * 512,
                              [[D, P], [P * D, NCH], [1, 512]])
                inc(gp.dma_start(wo_sbuf[nd % 2][:, :], src),
                    f"sgw{nd % 2}", 16)
                ev[("wosl", av, nd)] = (f"sgw{nd % 2}",
                                        cnt(f"sgw{nd % 2}"))
        blk.gpsimd(f_gp)

        def f_t(tensor):
            wev(tensor, ("oga", av))
            wge(tensor, "sv", sv_entry)
            wge(tensor, "sev", e0)
            for nd in range(4):
                wev(tensor, ("wosl", av, nd))
                for tt in range(NTILE):
                    gi = nd * 8 + tt
                    if gi >= 2:
                        wge(tensor, "sev", e0 + gi - 1)
                    ins = None
                    for hc in range(NCH):
                        ins = tensor.matmul(
                            ps_s[gi % 2][:, :],
                            O_own[:, ((tt // 4) * 16 + hc) * 512 +
                                  (tt % 4) * P:
                                  ((tt // 4) * 16 + hc) * 512 + (tt % 4 + 1) * P],
                            wo_sbuf[nd % 2][:, hc * 512:(hc + 1) * 512],
                            start=(hc == 0), stop=(hc == NCH - 1))
                    inc(ins, "stn")
        blk.tensor(f_t)

        def f_v(v):
            for nd in range(4):
                for tt in range(NTILE):
                    gi = nd * 8 + tt
                    wge(v, "stn", t0 + gi + 1)
                    inc(v.scalar_tensor_tensor(
                        x_sb[:, tt * D + nd * 512:tt * D + (nd + 1) * 512],
                        ps_s[gi % 2][:, :], 1.0,
                        x_sb[:, tt * D + nd * 512:tt * D + (nd + 1) * 512],
                        ALU.mult, ALU.add), "sev")
                    if nd == 3:
                        ev[("xf", lx, tt)] = ("sev", cnt("sev"))
        blk.vector(f_v)
        ev[("wodone", av)] = ("stn", cnt("stn"))

    def gating_phase(fi, lnf):
        for tt in range(NTILE):
            def f_t1(tensor, tt=tt):
                if tt == 0:
                    wev(tensor, ("lnslab", fi))
                    wev(tensor, "g32")
                else:
                    wev(tensor, ("grk", fi, tt - 1))
                ins = None
                for cc in range(NCH):
                    ins = tensor.matmul(
                        ps_r[:, 0:32],
                        lnf[:, cc * TCORE + tt * P:cc * TCORE + tt * P + P],
                        gwsb[:, cc * 32:(cc + 1) * 32],
                        start=(cc == 0), stop=(cc == NCH - 1))
                inc(ins, "stn")
                ev[("glog", fi, tt)] = ("stn", cnt("stn"))
            blk.tensor(f_t1)

            def f_v1(v, tt=tt):
                wev(v, ("glog", fi, tt))
                inc(v.reduce_max(gs2[:, 0:1], ps_r[:, 0:32], axis=AX.X), "sv")
                wge(v, "sv", cnt("sv"))
                inc(v.tensor_scalar(gs3[:, 0:1], gs2[:, 0:1], -1.0, None,
                                    ALU.mult), "sv")
                ev[("gnm", fi, tt)] = ("sv", cnt("sv"))
            blk.vector(f_v1)

            def f_s1(sc, tt=tt):
                wev(sc, ("gnm", fi, tt))
                inc(sc.activation(probs[:, :], ps_r[:, 0:32], AF.Exp,
                                  bias=gs3[:, 0:1], accum_out=gs1[:, 0:1]),
                    "sa")
                ev[("gexp", fi, tt)] = ("sa", cnt("sa"))
            blk.scalar(f_s1)

            def f_v2(v, tt=tt):
                wev(v, ("gexp", fi, tt))
                inc(v.reciprocal(gs4[:, 0:1], gs1[:, 0:1]), "sv")
                wge(v, "sv", cnt("sv"))
                inc(v.tensor_scalar(probs[:, :], probs[:, :], gs4[:, 0:1],
                                    None, ALU.mult), "sv")
                wge(v, "sv", cnt("sv"))
                in_j = bass.AP(probs, 0, [[32, P], [1, 32], [0, 32]])
                in_e = bass.AP(probs, 0, [[32, P], [0, 32], [1, 32]])
                o_c3 = bass.AP(cmp_sc, 0, [[1024, P], [32, 32], [1, 32]])
                inc(v.tensor_tensor(o_c3, in_j, in_e, ALU.is_gt), "sv")
                wge(v, "sv", cnt("sv"))
                in_r = bass.AP(cmp_sc, 0, [[1024, P], [1, 32], [32, 32]])
                o_r = bass.AP(rank_sb, 0, [[32, P], [1, 32], [0, 1]])
                inc(v.reduce_sum(o_r, in_r, axis=AX.X), "sv")
                wge(v, "sv", cnt("sv"))
                inc(v.tensor_single_scalar(sel_sb[:, :], rank_sb[:, :], 8.0,
                                           ALU.is_lt), "sv")
                wge(v, "sv", cnt("sv"))
                inc(v.tensor_mul(mp_sb[:, :], probs[:, :], sel_sb[:, :]), "sv")
                ev[("gmp", fi, tt)] = ("sv", cnt("sv"))
            blk.vector(f_v2)

            def f_t2(tensor, tt=tt):
                wev(tensor, ("gmp", fi, tt))
                if tt > 0:
                    wev(tensor, ("gwr2", fi, tt - 1))
                ins = tensor.transpose(ps_t[0:32, 0:P], mp_sb[:, 0:32],
                                       ident[:, :])
                inc(ins, "stn")
                ev[("gmpt", fi, tt)] = ("stn", cnt("stn"))
            blk.tensor(f_t2)

            def f_v3(v, tt=tt):
                wev(v, ("gmpt", fi, tt))
                inc(v.tensor_copy(mp_fm[0:32, 0:P], ps_t[0:32, 0:P]), "sv")
                ev[("gmpf", fi, tt)] = ("sv", cnt("sv"))
            blk.vector(f_v3)

            def f_t3(tensor, tt=tt):
                wev(tensor, ("gmpf", fi, tt))
                wev(tensor, ("glog", fi, tt))
                wev(tensor, ("gexp", fi, tt))
                ins = tensor.matmul(ps_r[:, 0:8], mp_fm[0:32, 0:P],
                                    G32[0:32, 0:8], start=True, stop=True)
                inc(ins, "stn")
                ev[("gwrm", fi, tt)] = ("stn", cnt("stn"))
            blk.tensor(f_t3)

            def f_v4(v, tt=tt):
                wev(v, ("gwrm", fi, tt))
                inc(v.tensor_copy(wraw[:, :], ps_r[:, 0:8]), "sv")
                ev[("grk", fi, tt)] = ("sv", cnt("sv"))
                wge(v, "sv", cnt("sv"))
                inc(v.reduce_sum(gs1[:, 0:1], wraw[:, :], axis=AX.X), "sv")
                wge(v, "sv", cnt("sv"))
                inc(v.reciprocal(gs4[:, 0:1], gs1[:, 0:1]), "sv")
                wge(v, "sv", cnt("sv"))
                inc(v.tensor_scalar(wtg_all[:, tt * 8:(tt + 1) * 8],
                                    wraw[:, :], gs4[:, 0:1], None, ALU.mult),
                    "sv")
                ev[("gwr2", fi, tt)] = ("sv", cnt("sv"))
            blk.vector(f_v4)
        ev[("gdone", fi)] = ("sv", cnt("sv"))

    def ffn_phase(fi, lnf, act_fm, dslab, gu_tl, fsc, lx, gate_loads):
        w_half, wbc, upsc, silu1 = fsc
        for h in range(2):
            t0 = cnt("stn")
            def fmm_val(ld, t0=t0):
                return t0 + (ld // 12) * 14 + 2 + (ld % 12) + 1

            def f_sy_gu(sync, h=h, t0=t0):
                ld = 0
                for g in range(8):
                    for ic in range(6):
                        for up in range(2):
                            oc = ic + 6 * up
                            if ld == 0:
                                wev(sync, "ag_gu")
                                if gate_loads is not None:
                                    wge(sync, gate_loads[0], gate_loads[1])
                            if ld >= 2:
                                wge(sync, "stn", fmm_val(ld - 2))
                            if ld < 2 and h == 1:
                                wge(sync, "stn", t0)
                            src = bass.AP(gu_all.tensor,
                                          g * D * 1536 + oc * P,
                                          [[1536, P], [P * 1536, NCH],
                                           [1, P]])
                            inc(sync.dma_start(gu_tl[ld % 2][:, :], src),
                                f"sf{ld % 2}", 16)
                            ev[("gusl", fi, h, ld)] = (f"sf{ld % 2}",
                                                       cnt(f"sf{ld % 2}"))
                            ld += 1
            blk.sync(f_sy_gu)

            e0, a0, v0 = cnt("sev"), cnt("sa"), cnt("sv")

            def f_t_gu(tensor, h=h, t0=t0, e0=e0, a0=a0, v0=v0):
                ld = 0
                for g in range(8):
                    if g == 0:
                        wev(tensor, ("gwr2", fi, h * 4 + 3))
                    if h == 0 and g == 0:
                        wev(tensor, ("lnslab", fi))
                    wge(tensor, "sv", v0 + g * 3)
                    ins = None
                    for t2 in range(4):
                        col = (h * 4 + t2) * 8 + g
                        ins = tensor.transpose(
                            bass.AP(ps_r, t2 * P, [[512, 1], [1, P]]),
                            wtg_all[:, col:col + 1], identf[:, :])
                    inc(ins, "stn")
                    wge(tensor, "sv", v0 + g * 3 + 1)
                    ins = tensor.matmul(ps_o[1][:, :], ones_rf[0:1, 0:P],
                                        w_half[0:1, :], start=True, stop=True)
                    inc(ins, "stn")
                    for ic in range(6):
                        for up in range(2):
                            wev(tensor, ("gusl", fi, h, ld))
                            i_g = g * 6 + ic
                            if up == 0 and i_g >= 1:
                                wge(tensor, "sa", a0 + i_g)
                            if up == 1 and i_g >= 1:
                                wge(tensor, "sev", e0 + (i_g - 1) * 2 + 1)
                            ins = None
                            for cc in range(NCH):
                                ins = tensor.matmul(
                                    ps_s[up][:, :],
                                    gu_tl[ld % 2][:, cc * P:(cc + 1) * P],
                                    lnf[:, cc * TCORE + h * 512:
                                        cc * TCORE + (h + 1) * 512],
                                    start=(cc == 0), stop=(cc == NCH - 1))
                            inc(ins, "stn")
                            ld += 1
            blk.tensor(f_t_gu)



            def f_s_gu(sc, h=h, t0=t0, e0=e0):
                for g in range(8):
                    for ic in range(6):
                        i_g = g * 6 + ic
                        wge(sc, "stn", t0 + g * 14 + 2 + 2 * ic + 1)
                        if i_g >= 1:
                            wge(sc, "sev", e0 + (i_g - 1) * 2 + 2)
                        inc(sc.activation(silu1[:, :], ps_s[0][:, :],
                                          AF.Silu), "sa")
            blk.scalar(f_s_gu)

            def f_v_gu(v, h=h, t0=t0, a0=a0, v0=v0, e0=e0):
                for g in range(8):
                    wge(v, "stn", t0 + g * 14 + 1)
                    inc(v.tensor_copy(w_half[0:1, :],
                                      bass.AP(ps_r, 0,
                                              [[512, 1], [1, 512]])), "sv")
                    wge(v, "stn", t0 + g * 14 + 2)
                    inc(v.tensor_copy(wbc[:, :], ps_o[1][:, :]), "sv")
                    inc(v.engine_nop(), "sv")
                    wge(v, "sv", v0 + g * 3 + 2)
                    for ic in range(6):
                        i_g = g * 6 + ic
                        wge(v, "stn", t0 + g * 14 + 2 + 2 * ic + 2)
                        inc(v.tensor_mul(upsc[:, :], ps_s[1][:, :],
                                         wbc[:, :]), "sev")
                        wge(v, "sev", e0 + i_g * 2 + 1)
                        wge(v, "sa", a0 + i_g + 1)
                        inc(v.tensor_mul(
                            act_fm[:, (g * 6 + ic) * 512:
                                   (g * 6 + ic + 1) * 512],
                            silu1[:, :], upsc[:, :]), "sev")
            blk.vector(f_v_gu)

            t0d = cnt("stn") + 16 * 14 - t0 + t0
            t0d = t0 + 8 * 14
            acts_done = e0 + 96

            def f_sy_dn(sync, h=h, t0d=t0d, fi=fi):
                for hc in range(8):
                    if hc >= 2:
                        wge(sync, "stn", t0d + (hc - 1) * 4)
                    if hc < 2:
                        wge(sync, "stn", t0d)
                        wev(sync, "ag_dn")
                    if hc < 2 and gate_loads is not None:
                        wge(sync, gate_loads[0], gate_loads[1])
                    src = bass.AP(dn_all.tensor, hc * 256 + h * 0,
                                  [[D, P], [P * D, 48], [1, 256]])
                    inc(sync.dma_start(dslab[hc % 2][:, :], src),
                        f"sd{hc % 2}", 16)
                    ev[("dnsl", fi, h, hc)] = (f"sd{hc % 2}",
                                               cnt(f"sd{hc % 2}"))
            blk.sync(f_sy_dn)

            e0d = cnt("sev")

            def f_t_dn(tensor, h=h, t0d=t0d, e0d=e0d, acts_done=acts_done):
                for hc in range(8):
                    wev(tensor, ("dnsl", fi, h, hc))
                    if hc == 0:
                        wge(tensor, "sev", acts_done)
                    for t2 in range(4):
                        gi = hc * 4 + t2
                        if gi >= 2:
                            wge(tensor, "sev", e0d + gi - 1)
                        ins = None
                        for g in range(8):
                            for ic in range(6):
                                k = g * 6 + ic
                                ins = tensor.matmul(
                                    bass.AP(ps_s[gi % 2], 0,
                                            [[512, P], [1, 256]]),
                                    act_fm[:, k * 512 + t2 * P:
                                           k * 512 + (t2 + 1) * P],
                                    dslab[hc % 2][:, k * 256:(k + 1) * 256],
                                    start=(k == 0), stop=(k == 47))
                        inc(ins, "stn")
            blk.tensor(f_t_dn)

            def f_v_dn(v, h=h, t0d=t0d, lx=lx):
                for hc in range(8):
                    for t2 in range(4):
                        gi = hc * 4 + t2
                        wge(v, "stn", t0d + gi + 1)
                        tt = h * 4 + t2
                        xo = x_sb[:, tt * D + hc * 256:
                                  tt * D + (hc + 1) * 256]
                        inc(v.scalar_tensor_tensor(
                            xo, bass.AP(ps_s[gi % 2], 0,
                                        [[512, P], [1, 256]]),
                            1.0, xo, ALU.mult, ALU.add), "sev")
                        if hc == 7:
                            ev[("xf", lx, tt)] = ("sev", cnt("sev"))
            blk.vector(f_v_dn)
        ev[("ffndone", fi)] = ("stn", cnt("stn"))

    sW0 = ExitStack()
    wqkvA = sbuf(sW0, "wqkvA", [P, QO + 16 * 256], BF16)
    sGth = ExitStack()
    gath32 = [sbuf(sGth, f"gath32_{i}", [P, D], F32) for i in range(2)]
    gath16 = [sbuf(sGth, f"gath16_{i}", [P, D], BF16) for i in range(2)]
    idx2 = [sbuf(sGth, f"idx2_{i}", [P, 1], I32) for i in range(2)]

    lnS = ExitStack()
    scr = alloc_ln_scratch(lnS, "A")
    ln_emit(0, [ev[("xload", tt)] for tt in range(NTILE)], scr, ln_cd[0])
    lnS.close()

    def mk_gp_ag_ln(li, av):
        def gp_ag_ln(gp):
            for h in range(2):
                for s, v in ev[("lnw", li, h)]:
                    wge(gp, s, v)
                inc(gp.collective_compute(
                    "AllGather", ALU.bypass,
                    replica_groups=[[0, 1, 2, 3], [4, 5, 6, 7]],
                    ins=[bass.AP(ln_cd[av].tensor, h * 512 * D,
                                 [[D, 512], [1, D]])],
                    outs=[bass.AP(ln_alld[av].tensor, h * 2048 * D,
                                  [[D, 2048], [1, D]])]), "scc")
                ev[("ag_ln", av, h)] = ("scc", cnt("scc"))
        return gp_ag_ln

    load_wqkv(0, wqkvA, 1, wq_a, wk_a, wv_a, None)
    blk.gpsimd(mk_gp_ag_ln(0, 0))

    def gp_dom(gp):
        wev(gp, "consts")
        inc(gp.indirect_dma_start(
            dom_sb[:, 0:1], None, dome_in[:, :],
            bass.IndirectOffsetOnAxis(cvec_sb[:, 0:1], 0)), "sgd", 16)
        ev["dom"] = ("sgd", cnt("sgd"))
        inc(gp.iota(iota_p[:, 0:1], [[0, 1]], channel_multiplier=1), "sio", 1)
        ev["iota"] = ("sio", cnt("sio"))
    blk.gpsimd(gp_dom)

    chunks = [("gu", i) for i in range(12)] + [("dn", i) for i in range(16)]
    v0g, t0g, e0g, w0g = (cnt("sv"), cnt("stn"), cnt("sev"), cnt("swc"))
    gg0 = [cnt("sgg0"), cnt("sgg1")]
    ngroups = []
    for i, (kind, ccc) in enumerate(chunks):
        nblk = 16 if kind == "gu" else 6
        for g in range((nblk + 7) // 8):
            ngroups.append((i, g, kind, ccc, min(8, nblk - g * 8)))
    cumg = [0]
    for kind, ccc in chunks:
        cumg.append(cumg[-1] + (2 if kind == "gu" else 1))
    grp_of_chunk = [[] for _ in chunks]
    for n, (i, g, kind, ccc, qn) in enumerate(ngroups):
        grp_of_chunk[i].append((n, qn))

    def gev(v, n, qn):
        wge(v, "stn", t0g + n + 1)
        if n >= 1:
            wge(v, "swc", w0g + n * 16)
        i_ap = bass.AP(ps_t, 0, [[1024, P], [P, qn], [1, P]])
        o_ap = bass.AP(cmp_sc, 0, [[1024, P], [P, qn], [1, P]])
        inc(v.tensor_copy(o_ap, i_ap), "sev")

    def f_v_idx(v):
        for i, (kind, ccc) in enumerate(chunks):
            if i == 0:
                wev(v, "iota")
                wev(v, "dom")
            rows = 1536 if kind == "gu" else D
            wid = D if kind == "gu" else 768
            if i >= 2:
                wge(v, f"sgg{i % 2}", gg0[i % 2] + 16 * (i // 2))
            inc(v.tensor_scalar(idx2[i % 2][:, 0:1], dom_sb[:, 0:1], rows,
                                ccc * P, ALU.mult, ALU.add), "sv")
            wge(v, "sv", v0g + 3 * i + 1)
            inc(v.tensor_add(idx2[i % 2][:, 0:1], idx2[i % 2][:, 0:1],
                             iota_p[:, 0:1]), "sv")
            wge(v, f"sgg{i % 2}",
                gg0[i % 2] + 16 * (i // 2 + 1))
            if i >= 1:
                for n, qn in grp_of_chunk[i - 1]:
                    gev(v, n, qn)
                wge(v, "stn", t0g + cumg[i])
            inc(v.tensor_copy(gath16[i % 2][:, 0:wid],
                              gath32[i % 2][:, 0:wid]), "sv")
        for n, qn in grp_of_chunk[len(chunks) - 1]:
            gev(v, n, qn)
        ev["gath_done"] = ("sev", cnt("sev"))
    blk.vector(f_v_idx)

    def gout(gp, n, qn, kind, ccc, g):
        wge(gp, "sev", e0g + n + 1)
        dstd = guc if kind == "gu" else dnc
        wid = 1536 if kind == "gu" else D
        d_ap = bass.AP(dstd.tensor, (g * 8) * P * wid + ccc * P,
                       [[wid, P], [P * wid, qn], [1, P]])
        s_ap = bass.AP(cmp_sc, 0, [[1024, P], [P, qn], [1, P]])
        inc(gp.dma_start(d_ap, s_ap), "swc", 16)

    def f_gp_gather(gp):
        for i, (kind, ccc) in enumerate(chunks):
            wid = D if kind == "gu" else 768
            srcd = gu_in if kind == "gu" else dn_in
            wge(gp, "sv", v0g + 3 * i + 2)
            if i >= 2:
                wge(gp, "sv", v0g + 3 * (i - 1))
            inc(gp.indirect_dma_start(
                gath32[i % 2][:, 0:wid], None, srcd[:, :],
                bass.IndirectOffsetOnAxis(idx2[i % 2][:, 0:1], 0)),
                f"sgg{i % 2}", 16)
            if i >= 1:
                kp, cp = chunks[i - 1]
                for (n, qn) in grp_of_chunk[i - 1]:
                    gg = n - grp_of_chunk[i - 1][0][0]
                    gout(gp, n, qn, kp, cp, gg)
        kp, cp = chunks[-1]
        for (n, qn) in grp_of_chunk[len(chunks) - 1]:
            gg = n - grp_of_chunk[len(chunks) - 1][0][0]
            gout(gp, n, qn, kp, cp, gg)
        ev["gwr"] = ("swc", cnt("swc"))
    blk.gpsimd(f_gp_gather)

    def f_t_g(tensor):
        for n, (i, g, kind, ccc, qn) in enumerate(ngroups):
            wge(tensor, "sv", v0g + 3 * i + 3)
            if n >= 1:
                wge(tensor, "sev", e0g + n)
            ins = None
            for q in range(qn):
                cc = g * 8 + q
                ins = tensor.transpose(ps_t[:, q * P:(q + 1) * P],
                                       gath16[i % 2][:, cc * P:(cc + 1) * P],
                                       ident[:, :])
            inc(ins, "stn")
    blk.tensor(f_t_g)


    def gp_ag_w(gp):
        wev(gp, "gwr")
        inc(gp.collective_compute(
            "AllGather", ALU.bypass, replica_groups=[list(range(8))],
            ins=[guc.opt()], outs=[gu_all.opt()]), "scc")
        ev["ag_gu"] = ("scc", cnt("scc"))
        inc(gp.collective_compute(
            "AllGather", ALU.bypass, replica_groups=[list(range(8))],
            ins=[dnc.opt()], outs=[dn_all.opt()]), "scc")
        ev["ag_dn"] = ("scc", cnt("scc"))
    blk.gpsimd(gp_ag_w)

    sGth.close()

    sA = ExitStack()
    alibi_kk = sbuf(sA, "alibi_kk", [6, TBATCH], BF16)
    alibi_qq = sbuf(sA, "alibi_qq", [6, TBATCH], BF16)
    masks = sbuf(sA, "masks", [P, 4 * 512], BF16)


    ascr['ptil'] = [sbuf(sA, f"ptilA{i}", [P, 512], BF16) for i in range(4)]
    ascr['sm'] = sbuf(sA, "smA", [P, 512], F32)
    ascr['oev'] = sbuf(sA, "oevA", [P, 512], BF16)
    ascr['r'] = sbuf(sA, "rA", [1, 512], F32)
    ascr['ost'] = [sbuf(sA, f"ostA{i}", [P, 512], BF16) for i in range(2)]
    Q_fmA = sbuf(sA, "Q_fmA", [P, 4 * TBATCH], BF16)
    K_fmA = sbuf(sA, "K_fmA", [P, TBATCH], BF16)
    V_tmA = sbuf(sA, "V_tmA", [P, TBATCH], BF16)
    sA3 = ExitStack()
    lslabA = [sbuf(sA3, f"lslabA{i}", [P, 16 * 512], BF16) for i in range(2)]
    qkv_phase(0, wqkvA, lslabA, Q_fmA, K_fmA, V_tmA, ln_alld[0], 1, None)
    sA3.close()
    def ld_amask(sync):
        wev(sync, "gath_done")
        inc(sync.dma_start(masks[:, :], masks_in[:, :]), "sxm", 16)
        inc(sync.dma_start(alibi_kk[0:6, :], alibi_in[0:6, :]), "sxm", 16)
        inc(sync.dma_start(alibi_qq[0:6, :], alibi_in[6:12, :]), "sxm", 16)
        ev["amask"] = ("sxm", cnt("sxm"))
    blk.sync(ld_amask)

    def gate_attn_amask(tensor):
        wev(tensor, "amask")
    blk.tensor(gate_attn_amask)

    attn_phase(0, Q_fmA, K_fmA, V_tmA, 1, None, o_cd[0])
    _SW0_CLOSE_MARK = True
    if DBG:
        def f_dbgq(sync):
            s, v = ev[("qkvdone", 0)]
            wge(sync, s, v)
            inc(sync.dma_start(dbg_qfm[:, :], Q_fmA[:, :]), "sw", 16)
            inc(sync.dma_start(dbg_kfm[:, :], K_fmA[:, :]), "sw", 16)
            inc(sync.dma_start(dbg_vtm[:, :], V_tmA[:, :]), "sw", 16)
        blk.sync(f_dbgq)
    sA.close()

    sW = ExitStack()
    O_ownA = sbuf(sW, "O_ownA", [P, 32 * 512], BF16)
    wo_sbA = [sbuf(sW, f"wo_sbA{i}", [P, 16 * 512], BF16) for i in range(2)]
    wo_phase(0, O_ownA, wo_sbA, wo_a, 1)
    if DBG:
        def f_dbgo(sync):
            wge(sync, "stn", cnt("stn"))
            inc(sync.dma_start(dbg_oown[:, :], O_ownA[:, :]), "sw", 16)
            inc(sync.dma_start(dbg_wosb[:, :], wo_sbA[1][:, :]), "sw", 16)
        blk.sync(f_dbgo)
    sW.close()
    sW0.close()

    if UPTO >= 2:
        sF = ExitStack()
        ln2_fm = sbuf(sF, "ln2_fm", [P, NCH * TCORE], BF16)
        sL2 = ExitStack()
        scr2 = alloc_ln_scratch(sL2, "B")
        ln_to_slab(0, 10, [ev[("xf", 1, tt)] for tt in range(NTILE)],
                   scr2, ln2_fm, gate_first=ev[("wodone", 0)])
        sL2.close()
        gating_phase(0, ln2_fm)
        sFC = ExitStack()
        act_fm1 = sbuf(sFC, "act_fm1", [P, 48 * 512], BF16)
        sh48a = sbuf(sFC, "sh48a", [P, 2 * 48 * 256], BF16)
        dslab1 = [sh48a[:, i * 48 * 256:(i + 1) * 48 * 256] for i in range(2)]
        gu_tl1 = [sh48a[:, i * 2048:(i + 1) * 2048] for i in range(2)]
        fsc1 = (sbuf(sFC, "w_half1", [1, 512], F32),
                sbuf(sFC, "wbc1", [P, 512], F32),
                sbuf(sFC, "upsc1", [P, 512], BF16),
                sbuf(sFC, "silu1a", [P, 512], BF16))
        ffn_phase(0, ln2_fm, act_fm1, dslab1, gu_tl1, fsc1, 2,
                  ev[("wodone", 0)])
        sFC.close()
        sF.close()

    if UPTO >= 3:
        sL3 = ExitStack()
        scr3 = alloc_ln_scratch(sL3, "C")
        ln_emit(1, [ev[("xf", 2, tt)] for tt in range(NTILE)], scr3,
                ln_cd[1], gate_first=ev[("ffndone", 0)])
        sL3.close()
        blk.gpsimd(mk_gp_ag_ln(1, 1))

        sB = ExitStack()
        ascr['ptil'] = [sbuf(sB, f"ptilB{i}", [P, 512], BF16)
                        for i in range(4)]
        ascr['oev'] = sbuf(sB, "oevB", [P, 512], BF16)
        ascr['r'] = sbuf(sB, "rB", [1, 512], F32)
        ascr['ost'] = [sbuf(sB, f"ostB{i}", [P, 512], BF16) for i in range(2)]
        Q_fmB = sbuf(sB, "Q_fmB", [P, 4 * TBATCH], BF16)
        K_fmB = sbuf(sB, "K_fmB", [P, 2 * TBATCH], BF16)
        V_tmB = sbuf(sB, "V_tmB", [P, 2 * TBATCH], BF16)
        sB3 = ExitStack()
        wqkvB = sbuf(sB3, "wqkvB", [P, QO + 16 * 512], BF16)
        lslabB = [sbuf(sB3, f"lslabB{i}", [P, 16 * 512], BF16)
                  for i in range(2)]
        load_wqkv(1, wqkvB, 2, wq_b, wk_b, wv_b, ev[("ffndone", 0)])
        qkv_phase(1, wqkvB, lslabB, Q_fmB, K_fmB, V_tmB, ln_alld[1], 2,
                  ev[("ffndone", 0)])
        sB3.close()
        attn_phase(1, Q_fmB, K_fmB, V_tmB, 2, None, o_cd[1])
        sB.close()

        sW2 = ExitStack()
        O_ownB = sbuf(sW2, "O_ownB", [P, 32 * 512], BF16)
        wo_sbB = [sbuf(sW2, f"wo_sbB{i}", [P, 16 * 512], BF16)
                  for i in range(2)]
        wo_phase(1, O_ownB, wo_sbB, wo_b, 3)
        sW2.close()

    if UPTO >= 4:
        sF2 = ExitStack()
        ln4_fm = sbuf(sF2, "ln4_fm", [P, NCH * TCORE], BF16)
        sL4 = ExitStack()
        scr4 = alloc_ln_scratch(sL4, "Dd")
        ln_to_slab(1, 11, [ev[("xf", 3, tt)] for tt in range(NTILE)],
                   scr4, ln4_fm, gate_first=ev[("wodone", 1)])
        sL4.close()
        gating_phase(1, ln4_fm)
        sFC2 = ExitStack()
        act_fm2 = sbuf(sFC2, "act_fm2", [P, 48 * 512], BF16)
        sh48b = sbuf(sFC2, "sh48b", [P, 2 * 48 * 256], BF16)
        dslab2 = [sh48b[:, i * 48 * 256:(i + 1) * 48 * 256] for i in range(2)]
        gu_tl2 = [sh48b[:, i * 2048:(i + 1) * 2048] for i in range(2)]
        fsc2 = (sbuf(sFC2, "w_half2", [1, 512], F32),
                sbuf(sFC2, "wbc2", [P, 512], F32),
                sbuf(sFC2, "upsc2", [P, 512], BF16),
                sbuf(sFC2, "silu2a", [P, 512], BF16))
        ffn_phase(1, ln4_fm, act_fm2, dslab2, gu_tl2, fsc2, 4,
                  ev[("wodone", 1)])
        sFC2.close()
        sF2.close()

    out_lx = UPTO

    def f_out(sync):
        for tt in range(NTILE):
            s, vv = ev[("xf", out_lx, tt)]
            wge(sync, s, vv)
            inc(sync.dma_start(x_out[tt * P:(tt + 1) * P, :],
                               x_sb[:, tt * D:(tt + 1) * D]), "sw", 16)
        if DBG:
            inc(sync.dma_start(dbg_lnall[:, :], ln_alld[0][:, :]), "sw", 16)
            inc(sync.dma_start(dbg_oall[:, :], o_alld[0][:, :]), "sw", 16)
        wge(sync, "sw", cnt("sw"))
    blk.sync(f_out)

    top.close()
    _PROG_CACHE["p"] = nc
    return nc


def _split3(v):
    v = np.asarray(v, np.float32)
    hi = v.astype(BF)
    mid = (v - hi.astype(np.float32)).astype(BF)
    lo = (v - hi.astype(np.float32) - mid.astype(np.float32)).astype(BF)
    return hi, mid, lo


def _host_inputs(inputs, c):
    b, j = c // 4, c % 4
    x = inputs["L0_attn_a_x"]
    Wq, Wk, Wv, Wo = (inputs["L0_attn_a_Wq"], inputs["L0_attn_a_Wk"],
                      inputs["L0_attn_a_Wv"], inputs["L0_attn_a_Wo"])
    wqb, wkb, wvb, wob = (inputs["L0_attn_b_wq"], inputs["L0_attn_b_wk"],
                          inputs["L0_attn_b_wv"], inputs["L0_attn_b_wo"])
    gwe = inputs["L0_ffn_gate_weight"]
    gup = inputs["L0_ffn_gate_up_proj"]
    dwn = inputs["L0_ffn_down_proj"]
    mg = np.asarray(inputs["L0_ffn_merge_groups"], np.int32).reshape(32, 1)
    dome = np.asarray(inputs["L0_ffn_dominant_experts"],
                      np.int32).reshape(8, 1)
    C = np.ascontiguousarray
    d = {
        "x_in": C(x[b, j * 1024:(j + 1) * 1024, :]),
        "wq_a": C(Wq[:, 4 * j * 128:(4 * j + 4) * 128]),
        "wk_a": C(Wk[:, j * 128:(j + 1) * 128]),
        "wv_a": C(Wv[:, j * 128:(j + 1) * 128]),
        "wo_a": C(Wo),
        "wq_b": C(wqb[4 * j * 128:(4 * j + 4) * 128, :].T),
        "wk_b": C(wkb[2 * j * 128:(2 * j + 2) * 128, :].T),
        "wv_b": C(wvb[2 * j * 128:(2 * j + 2) * 128, :].T),
        "wo_b": C(wob.T),
        "gw_t": C(gwe.T),
        "gu_in": C(gup.reshape(32 * 1536, D)),
        "dn_in": C(dwn.reshape(32 * D, 768)),
        "mg_in": mg,
        "dome_in": dome,
        "cvec_in": np.full((P, 1), c, np.int32),
        "ident_in": np.eye(P, dtype=np.float32).astype(BF),
        "identf_in": np.eye(P, dtype=np.float32),
        "cbias_in": np.tile(np.array([[LN_EPS, -M_STAB]], np.float32),
                            (P, 1)),
        "agw_in": np.array([[6 + 2 * j, 16 + 2 * j]], np.int32),
    }
    m = np.zeros((P, 4 * 512), np.float32)
    ql = np.arange(512)
    for di in range(4):
        kl = np.arange(P)[:, None] + di * 128
        m[:, di * 512:(di + 1) * 512] = np.where(kl <= ql[None, :],
                                                 0.0, -1e30)
    d["masks_in"] = m.astype(BF)
    slope = SL16[j]
    al = np.zeros((12, TBATCH), np.float32)
    kq = np.arange(TBATCH, dtype=np.float64)
    h, mi, lo = _split3(slope * kq)
    al[0, :] = h.astype(np.float32)
    al[1, :] = mi.astype(np.float32)
    al[2, :] = lo.astype(np.float32)
    al[3:6, :] = 1.0
    al[6:9, :] = 1.0
    h, mi, lo = _split3(-slope * kq)
    al[9, :] = h.astype(np.float32)
    al[10, :] = mi.astype(np.float32)
    al[11, :] = lo.astype(np.float32)
    d["alibi_in"] = al.astype(BF)
    oi = np.zeros((P, 32), np.int32)
    pcol = np.arange(P, dtype=np.int32)
    ti = 0
    for k in range(2):
        qc = 2 * j + k
        for r in range(4):
            for hl in range(4):
                oi[:, ti] = qc * D + r * 512 + hl * 128 + pcol
                ti += 1
    d["oaidx_in"] = oi
    return d


def kernel(**inputs):
    nc = build_program()
    in_maps = [_host_inputs(inputs, c) for c in range(8)]
    res = run_bass_kernel_spmd(nc, in_maps, core_ids=list(range(8)))
    x = inputs["L0_attn_a_x"]
    out = np.zeros((2, TBATCH, D), np.float32)
    for c in range(8):
        b, j = c // 4, c % 4
        out[b, j * 1024:(j + 1) * 1024, :] = res.results[c]["x_out"]
    return out


if __name__ == "__main__":
    build_program()
    print("build ok")


# version 05 (final)
# speedup vs baseline: 1.0299x; 1.0047x over previous
import math
import numpy as np
import ml_dtypes
from contextlib import ExitStack

import concourse.bass as bass
import concourse.mybir as mybir
from concourse.bass_utils import run_bass_kernel_spmd

F32 = mybir.dt.float32
BF16 = mybir.dt.bfloat16
I32 = mybir.dt.int32
AF = mybir.ActivationFunctionType
ALU = mybir.AluOpType
AX = mybir.AxisListType
BF = ml_dtypes.bfloat16

P = 128
D = 2048
NCH = 16
TCORE = 1024
TBATCH = 4096
NTILE = 8
NQC = 8
M_STAB = 16.0
ISQ = 1.0 / math.sqrt(128.0)
LN_EPS = 1e-5
UPTO = 4
DBG = False
QO = 16 * 512

SL16 = [2.0 ** (-0.5 * (i + 1)) for i in range(16)]

_PROG_CACHE = {}


def build_program():
    if "p" in _PROG_CACHE:
        return _PROG_CACHE["p"]
    nc = bass.Bass("TRN2", target_bir_lowering=False, num_devices=8,
                   dynamic_dma_scratch_size=16384)
    top = ExitStack()
    sems, cnts = {}, {}

    def sem(name):
        if name not in sems:
            sems[name] = top.enter_context(nc.semaphore(name))
            cnts[name] = 0
        return sems[name]

    def inc(ins, name, n=1):
        ins.then_inc(sem(name), n)
        cnts[name] += n
        return cnts[name]

    def cnt(name):
        sem(name)
        return cnts[name]

    def wge(eng, name, val):
        if val > 0:
            eng.wait_ge(sem(name), val)

    t = nc.dram_tensor
    x_in = t("x_in", [TCORE, D], F32, kind="ExternalInput").ap()
    wq_a = t("wq_a", [D, 512], F32, kind="ExternalInput").ap()
    wk_a = t("wk_a", [D, 128], F32, kind="ExternalInput").ap()
    wv_a = t("wv_a", [D, 128], F32, kind="ExternalInput").ap()
    wo_a = t("wo_a", [D, D], F32, kind="ExternalInput").ap()
    wq_b = t("wq_b", [D, 512], F32, kind="ExternalInput").ap()
    wk_b = t("wk_b", [D, 256], F32, kind="ExternalInput").ap()
    wv_b = t("wv_b", [D, 256], F32, kind="ExternalInput").ap()
    wo_b = t("wo_b", [D, D], F32, kind="ExternalInput").ap()
    gw_t = t("gw_t", [D, 32], F32, kind="ExternalInput").ap()
    gu_in = t("gu_in", [32 * 1536, D], F32, kind="ExternalInput").ap()
    dn_in = t("dn_in", [32 * D, 768], F32, kind="ExternalInput").ap()
    mg_in = t("mg_in", [32, 1], I32, kind="ExternalInput").ap()
    dome_in = t("dome_in", [8, 1], I32, kind="ExternalInput").ap()
    cvec_in = t("cvec_in", [P, 1], I32, kind="ExternalInput").ap()
    ident_in = t("ident_in", [P, P], BF16, kind="ExternalInput").ap()
    identf_in = t("identf_in", [P, P], F32, kind="ExternalInput").ap()
    masks_in = t("masks_in", [P, 4 * 512], BF16, kind="ExternalInput").ap()
    alibi_in = t("alibi_in", [12, TBATCH], BF16, kind="ExternalInput").ap()
    oaidx_in = t("oaidx_in", [P, 32], I32, kind="ExternalInput").ap()
    cbias_in = t("cbias_in", [P, 2], F32, kind="ExternalInput").ap()
    agw_in = t("agw_in", [1, 2], I32, kind="ExternalInput").ap()
    x_out = t("x_out", [TCORE, D], F32, kind="ExternalOutput").ap()
    if DBG:
        dbg_lnall = t("dbg_lnall", [4 * TCORE, D], BF16,
                      kind="ExternalOutput").ap()
        dbg_oall = t("dbg_oall", [8 * D, 512], BF16,
                     kind="ExternalOutput").ap()
        dbg_qfm = t("dbg_qfm", [P, 4 * TBATCH], BF16,
                    kind="ExternalOutput").ap()
        dbg_kfm = t("dbg_kfm", [P, TBATCH], BF16,
                    kind="ExternalOutput").ap()
        dbg_vtm = t("dbg_vtm", [P, TBATCH], BF16,
                    kind="ExternalOutput").ap()
        dbg_oown = t("dbg_oown", [P, 32 * 512], BF16,
                     kind="ExternalOutput").ap()
        dbg_wosb = t("dbg_wosb", [P, 16 * 512], BF16,
                     kind="ExternalOutput").ap()

    ln_cd = [t(f"ln_c{i}", [TCORE, D], BF16).ap() for i in range(2)]
    ln_alld = [t(f"ln_all{i}", [4 * TCORE, D], BF16).ap() for i in range(2)]
    ln_tm = [t(f"ln_tm{i}", [TCORE, D], BF16).ap() for i in range(2)]
    o_cd = [t(f"o_c{i}", [8 * 512, 512], BF16).ap() for i in range(2)]
    o_alld = [t(f"o_all{i}", [8 * D, 512], BF16).ap() for i in range(2)]
    guc = t("guc", [D, 1536], BF16).ap()
    gu_all = t("gu_all", [8 * D, 1536], BF16).ap()
    dnc = t("dnc", [768, D], BF16).ap()
    dn_all = t("dn_all", [8 * 768, D], BF16).ap()

    def sbuf(st_, name, shape, dt, **kw):
        return st_.enter_context(nc.sbuf_tensor(name, shape, dt, **kw))

    x_sb = sbuf(top, "x_sb", [P, NTILE * D], F32)
    ident = sbuf(top, "ident", [P, P], BF16)
    identf = sbuf(top, "identf", [P, P], F32)
    ones_col = sbuf(top, "ones_col", [P, 1], BF16)
    ones_rf = sbuf(top, "ones_rf", [1, P], F32)
    cvec_sb = sbuf(top, "cvec_sb", [P, 1], I32)
    mg_sb = sbuf(top, "mg_sb", [32, 1], I32)
    mg_f = sbuf(top, "mg_f", [32, 1], F32)
    G32 = sbuf(top, "G32", [32, 8], BF16)
    dom_sb = sbuf(top, "dom_sb", [P, 1], I32)
    iota_p = sbuf(top, "iota_p", [P, 1], I32)
    idx_sb = sbuf(top, "idx_sb", [P, 1], I32)
    oaidx_sb = sbuf(top, "oaidx_sb", [P, 32], I32)
    gwsb = sbuf(top, "gwsb", [P, 16 * 32], BF16)
    probs = sbuf(top, "probs", [P, 32], F32)
    cmp_sc = sbuf(top, "cmp_sc", [P, 1024], BF16)
    rank_sb = sbuf(top, "rank_sb", [P, 32], F32)
    sel_sb = sbuf(top, "sel_sb", [P, 32], F32)
    mp_sb = sbuf(top, "mp_sb", [P, 32], BF16)
    mp_fm = sbuf(top, "mp_fm", [32, P], BF16)
    wraw = sbuf(top, "wraw", [P, 8], F32)
    wtg_all = sbuf(top, "wtg_all", [P, 8 * NTILE], F32)
    gs1 = sbuf(top, "gs1", [P, 1], F32)
    gs2 = sbuf(top, "gs2", [P, 1], F32)
    gs3 = sbuf(top, "gs3", [P, 1], F32)
    gs4 = sbuf(top, "gs4", [P, 1], F32)
    cbias = sbuf(top, "cbias", [P, 2], F32)
    agw_sb = sbuf(top, "agw_sb", [1, 2], I32)

    ps_o = [top.enter_context(nc.psum_tensor(f"ps_o{i}", [P, 512], F32))
            for i in range(4)]
    ps_s = [top.enter_context(nc.psum_tensor(f"ps_s{i}", [P, 512], F32))
            for i in range(2)]
    ps_r = top.enter_context(nc.psum_tensor("ps_r", [P, 512], F32))
    ps_t = top.enter_context(nc.psum_tensor("ps_t", [P, 1024], BF16))

    blk = top.enter_context(nc.Block())
    ev = {}
    ascr = {}

    def wev(eng, tag):
        e = ev[tag]
        for s, v in (e if isinstance(e, list) else [e]):
            wge(eng, s, v)

    def ph0_sync(sync):
        for tt in range(NTILE):
            inc(sync.dma_start(x_sb[:, tt * D:(tt + 1) * D],
                               x_in[tt * P:(tt + 1) * P, :]), "sx", 16)
        for dst, src in [(ident, ident_in), (identf, identf_in),
                         (cvec_sb, cvec_in), (mg_sb, mg_in),
                         (oaidx_sb, oaidx_in), (cbias, cbias_in),
                         (agw_sb, agw_in)]:
            inc(sync.dma_start(dst[:, :], src[:, :]), "sx", 16)
        ev["consts"] = ("sx", cnt("sx"))
    blk.sync(ph0_sync)
    for tt in range(NTILE):
        ev[("xload", tt)] = ev["consts"]

    def ph0_gp(gp):
        src = bass.AP(gw_t.tensor, 0, [[32, P], [32 * P, NCH], [1, 32]])
        inc(gp.dma_start(gwsb[:, :], src), "sgd", 16)
        ev["gw"] = ("sgd", cnt("sgd"))
    blk.gpsimd(ph0_gp)

    def ph0_vec(v):
        v.memset(ones_col[:, :], 1.0)
        inc(v.memset(ones_rf[:, :], 1.0), "sv")
        ev["ones"] = ("sv", cnt("sv"))
        wev(v, "consts")
        inc(v.tensor_copy(mg_f[:, :], mg_sb[:, :]), "sv")
        wge(v, "sv", cnt("sv"))
        ins = None
        for g in range(8):
            ins = v.tensor_single_scalar(G32[:, g:g + 1], mg_f[:, 0:1],
                                         float(g), ALU.is_equal)
        inc(ins, "sv")
        ev["g32"] = ("sv", cnt("sv"))
    blk.vector(ph0_vec)

    def ln_emit(li, x_ready, scratch, dst_tm, gate_first=None):
        xsq, xhat, sts, tev = scratch
        (s_s, s_q, s_m, s_v, s_r, s_n, s_t2) = sts
        a0, v0 = cnt("sa"), cnt("sv")
        wl0 = [cnt(f"swl{i}") for i in range(4)]

        def f_sc(sc):
            for tt in range(NTILE):
                s, vv = x_ready[tt]
                wge(sc, s, vv)
                if tt == 0:
                    wev(sc, "consts")
                if tt == 0 and gate_first is not None:
                    wge(sc, gate_first[0], gate_first[1])
                pp = tt % 2
                xt = x_sb[:, tt * D:(tt + 1) * D]
                inc(sc.activation(xsq[:, :], xt, AF.Square,
                                  accum_out=s_q[pp][:, 0:1]), "sa")
                wge(sc, "sv", v0 + 6 * tt + 5)
                inc(sc.activation(s_v[pp][:, 0:1], s_v[pp][:, 0:1], AF.Ln,
                                  bias=cbias[:, 0:1]), "sa")
                wge(sc, "sa", a0 + 3 * tt + 2)
                inc(sc.activation(s_r[pp][:, 0:1], s_v[pp][:, 0:1], AF.Exp,
                                  scale=-0.5), "sa")
        blk.scalar(f_sc)

        def f_v(v):
            for tt in range(NTILE):
                s, vv = x_ready[tt]
                wge(v, s, vv)
                pp = tt % 2
                xt = x_sb[:, tt * D:(tt + 1) * D]
                inc(v.reduce_sum(s_s[pp][:, 0:1], xt, axis=AX.X), "sv")
                wge(v, "sv", v0 + 6 * tt + 1)
                inc(v.tensor_scalar(s_m[pp][:, 0:1], s_s[pp][:, 0:1],
                                    1.0 / D, None, ALU.mult), "sv")
                wge(v, "sv", v0 + 6 * tt + 2)
                wge(v, "sa", a0 + 3 * tt + 1)
                inc(v.scalar_tensor_tensor(s_t2[pp][:, 0:1], s_m[pp][:, 0:1],
                                           1.0, s_m[pp][:, 0:1],
                                           ALU.mult, ALU.mult), "sv")
                inc(v.tensor_scalar(s_v[pp][:, 0:1], s_q[pp][:, 0:1],
                                    1.0 / D, None, ALU.mult), "sv")
                wge(v, "sv", v0 + 6 * tt + 4)
                inc(v.tensor_sub(s_v[pp][:, 0:1], s_v[pp][:, 0:1],
                                 s_t2[pp][:, 0:1]), "sv")
                wge(v, "sa", a0 + 3 * tt + 3)
                if tt >= 2:
                    p4 = (tt - 2) % 4
                    wge(v, f"swl{p4}", wl0[p4] + 16 * ((tt - 2) // 4 + 1))
                inc(v.tensor_scalar(xhat[pp][:, :], xt, s_m[pp][:, 0:1],
                                    s_r[pp][:, 0:1], ALU.subtract, ALU.mult),
                    "sv")
        blk.vector(f_v)

        def f_sy(sync):
            for tt in range(NTILE):
                wge(sync, "sv", v0 + 6 * tt + 6)
                inc(sync.dma_start(dst_tm[tt * P:(tt + 1) * P, :],
                                   xhat[tt % 2][:, :]), f"swl{tt % 4}", 16)
                if tt == 3:
                    ev[("lnw", li, 0)] = [(f"swl{i}", cnt(f"swl{i}"))
                                          for i in range(4)]
            ev[("lnw", li, 1)] = [(f"swl{i}", cnt(f"swl{i}"))
                                  for i in range(4)]
        blk.sync(f_sy)

    def alloc_ln_scratch(st_, tag):
        xsq = sbuf(st_, f"xsq{tag}", [P, D], F32)
        xhat = [sbuf(st_, f"xh{tag}{i}", [P, D], BF16) for i in range(2)]
        sts = tuple([sbuf(st_, f"s{tag}{k}_{i}", [P, 1], F32)
                     for i in range(2)] for k in range(7))
        tev = [sbuf(st_, f"tv{tag}{i}", [P, 1024], BF16) for i in range(4)]
        return (xsq, xhat, sts, tev)

    def ln_to_slab(fi, li, x_ready, scratch, dst_slab, gate_first=None):
        ln_emit(li, x_ready, scratch, ln_tm[fi], gate_first=gate_first)

        def f_tr(sync):
            for s, v in ev[("lnw", li, 1)]:
                wge(sync, s, v)
            for cc in range(NCH):
                srcap = bass.AP(ln_tm[fi].tensor, cc * P,
                                [[D, TCORE], [1, P]])
                inc(sync.dma_start(dst_slab[:, cc * TCORE:(cc + 1) * TCORE],
                                   srcap, transpose=True), "slt", 16)
            ev[("lnslab", fi)] = ("slt", cnt("slt"))
        blk.sync(f_tr)

    def load_wqkv(av, wqkv, nkv, wq, wk, wv, gate_w):
        KVW = 256 * nkv
        KVH = 128 * nkv

        def f_gp_w(gp):
            if gate_w is not None:
                wge(gp, gate_w[0], gate_w[1])
            for cc in range(NCH):
                inc(gp.dma_start(wqkv[:, cc * 512:(cc + 1) * 512],
                                 wq[cc * P:(cc + 1) * P, :]), "swq", 16)
            for cc in range(NCH):
                for kv in range(nkv):
                    inc(gp.dma_start(
                        wqkv[:, QO + cc * KVW + kv * P:QO + cc * KVW + (kv + 1) * P],
                        wk[cc * P:(cc + 1) * P, kv * P:(kv + 1) * P]), "swq", 16)
                    inc(gp.dma_start(
                        wqkv[:, QO + cc * KVW + KVH + kv * P:
                             QO + cc * KVW + KVH + (kv + 1) * P],
                        wv[cc * P:(cc + 1) * P, kv * P:(kv + 1) * P]), "swq", 16)
            ev[("wqkv", av)] = ("swq", cnt("swq"))
        blk.gpsimd(f_gp_w)

    def qkv_phase(av, wqkv, lslab, Q_fm, K_fm, V_tm, lnall, nkv,
                  gate_slab):
        KVW = 256 * nkv
        KVH = 128 * nkv

        t0, a0, e0 = cnt("stn"), cnt("sev"), cnt("sa")
        t0, a0, e0 = cnt("stn"), cnt("sa"), cnt("sev")
        TPC = 4 + 3 * nkv
        APC = 4 + 2 * nkv

        def f_sy(sync):
            for tc in range(NQC):
                h, r = tc // 4, tc % 4
                if tc == 0 and gate_slab is not None:
                    wge(sync, gate_slab[0], gate_slab[1])
                if tc == 0 or tc == 4:
                    wev(sync, ("ag_ln", av, h))
                if tc >= 2:
                    wge(sync, "stn", t0 + (tc - 1) * TPC)
                for cc in range(NCH):
                    src = bass.AP(lnall.tensor,
                                  (h * 2048 + r * 512) * D + cc * P,
                                  [[D, 512], [1, P]])
                    inc(sync.dma_start(lslab[tc % 2][:, cc * 512:(cc + 1) * 512],
                                       src, transpose=True),
                        f"sl{tc % 2}", 16)
                ev[("lsl", av, tc)] = (f"sl{tc % 2}", cnt(f"sl{tc % 2}"))
        blk.sync(f_sy)

        def f_t(tensor):
            for tc in range(NQC):
                s, vv = ev[("lsl", av, tc)]
                wge(tensor, s, vv)
                if tc == 0:
                    wev(tensor, ("wqkv", av))
                    wge(tensor, "sev", e0)
                    wge(tensor, "sv", cnt("sv"))
                ls = lslab[tc % 2]
                for dq in range(4):
                    if tc >= 1:
                        wge(tensor, "sa", a0 + (tc - 1) * APC + dq + 1)
                    ins = None
                    for cc in range(NCH):
                        ins = tensor.matmul(
                            ps_o[dq][:, :],
                            wqkv[:, cc * 512 + dq * P:cc * 512 + (dq + 1) * P],
                            ls[:, cc * 512:(cc + 1) * 512],
                            start=(cc == 0), stop=(cc == NCH - 1))
                    inc(ins, "stn")
                for kv in range(nkv):
                    if tc >= 1:
                        wge(tensor, "sa", a0 + (tc - 1) * APC + 4 + kv + 1)
                    ins = None
                    for cc in range(NCH):
                        ins = tensor.matmul(
                            ps_s[kv][:, :],
                            wqkv[:, QO + cc * KVW + kv * P:
                                 QO + cc * KVW + (kv + 1) * P],
                            ls[:, cc * 512:(cc + 1) * 512],
                            start=(cc == 0), stop=(cc == NCH - 1))
                    inc(ins, "stn")
                for kv in range(nkv):
                    wge(tensor, "sa", a0 + tc * APC + 4 + kv + 1)
                    ins = None
                    for cc in range(NCH):
                        ins = tensor.matmul(
                            ps_s[kv][:, :],
                            wqkv[:, QO + cc * KVW + KVH + kv * P:
                                 QO + cc * KVW + KVH + (kv + 1) * P],
                            ls[:, cc * 512:(cc + 1) * 512],
                            start=(cc == 0), stop=(cc == NCH - 1))
                    inc(ins, "stn")
                for kv in range(nkv):
                    wge(tensor, "sa", a0 + tc * APC + 4 + nkv + kv + 1)
                    gi = tc * nkv + kv
                    if gi >= 1:
                        wge(tensor, "sev", e0 + gi)
                    ins = None
                    for q in range(4):
                        ins = tensor.transpose(
                            ps_t[:, q * P:(q + 1) * P],
                            ascr['ost'][kv % 2][:, q * P:(q + 1) * P],
                            ident[:, :])
                    inc(ins, "stn")
        blk.tensor(f_t)

        def f_sc(sc):
            for tc in range(NQC):
                h, r = tc // 4, tc % 4
                gt0 = r * 1024 + h * 512
                for dq in range(4):
                    wge(sc, "stn", t0 + tc * TPC + dq + 1)
                    inc(sc.activation(Q_fm[:, dq * TBATCH + gt0:
                                           dq * TBATCH + gt0 + 512],
                                      ps_o[dq][:, :], AF.Copy), "sa")
                for kv in range(nkv):
                    wge(sc, "stn", t0 + tc * TPC + 4 + kv + 1)
                    inc(sc.activation(K_fm[:, kv * TBATCH + gt0:
                                           kv * TBATCH + gt0 + 512],
                                      ps_s[kv][:, :], AF.Copy, scale=ISQ),
                        "sa")
                for kv in range(nkv):
                    wge(sc, "stn", t0 + tc * TPC + 4 + nkv + kv + 1)
                    gi = tc * nkv + kv
                    gprev = gi - (1 if nkv == 1 else 2)
                    if gprev >= 0:
                        tcp, kvp = divmod(gprev, nkv)
                        wge(sc, "stn", t0 + tcp * TPC + 4 + 2 * nkv + kvp + 1)
                    inc(sc.activation(ascr['ost'][kv % 2][:, :], ps_s[kv][:, :],
                                      AF.Copy), "sa")
        blk.scalar(f_sc)

        def f_vev(v):
            for tc in range(NQC):
                h, r = tc // 4, tc % 4
                gt0 = r * 1024 + h * 512
                for kv in range(nkv):
                    wge(v, "stn", t0 + tc * TPC + 4 + 2 * nkv + kv + 1)
                    i_ap = bass.AP(ps_t, 0, [[1024, P], [P, 4], [1, P]])
                    o_ap = bass.AP(V_tm, kv * TBATCH + gt0 * P // 128 * 128,
                                   [[nkv * TBATCH, P], [P, 4], [1, P]])
                    o_ap = bass.AP(V_tm, kv * TBATCH + gt0,
                                   [[nkv * TBATCH, P], [P, 4], [1, P]])
                    inc(v.tensor_copy(o_ap, i_ap), "sev")
        blk.vector(f_vev)
        ev[("qkvdone", av)] = ("stn", cnt("stn"))

    def attn_phase(av, Q_fm, K_fm, V_tm, nkv, alibi_kq, o_cdram):
        H = 4 // nkv
        causal = (av == 0)
        swp0 = [cnt(f"swp{i}") for i in range(4)]
        for qc in range(NQC):
            w_qc0 = cnt("sw")
            for kv in range(nkv):
                kbc = 4 * qc + 4 if causal else 32
                N = kbc * H
                t0, a0, v0 = cnt("stn"), cnt("sa"), cnt("sv")
                pv0, prv0, w0 = cnt("spv"), cnt("sprv"), cnt("sw")
                nfo = []
                for n in range(N):
                    kb, qh = divmod(n, H)
                    nfo.append((kb, qh, causal and kb >= 4 * qc))
                cdiag = [0]
                for (_, _, d) in nfo:
                    cdiag.append(cdiag[-1] + (1 if d else 0))
                prevdiag = [None] * max(N, 1)
                lastd = None
                for n2 in range(N):
                    prevdiag[n2] = lastd
                    if nfo[n2][2]:
                        lastd = n2

                def f_t(tensor, qc=qc, kv=kv, N=N, nfo=nfo, t0=t0, a0=a0,
                        v0=v0, pv0=pv0, prv0=prv0, cdiag=cdiag, kbc=kbc):
                    PB = 4 // H

                    def pvbatch(kb):
                        wge(tensor, "sa", a0 + (kb + 1) * H)
                        for qh in range(H):
                            oq = kv * H + qh
                            bi = (kb % PB) * H + qh
                            tensor.matmul(
                                ps_o[oq][:, :],
                                V_tm[:, kv * TBATCH + kb * P:
                                     kv * TBATCH + (kb + 1) * P],
                                ascr['ptil'][bi][:, :],
                                start=(kb == 0), stop=(kb == kbc - 1))
                        ins = None
                        for qh in range(H):
                            bi = (kb % PB) * H + qh
                            ins = tensor.matmul(
                                ps_r[32 * qh:32 * qh + 1, :],
                                ones_col[:, 0:1], ascr['ptil'][bi][:, :],
                                start=(kb == 0), stop=(kb == kbc - 1),
                                tile_position=(0, 32 * qh))
                        inc(ins, "spv")
                    for n in range(N):
                        kb, qh, diag = nfo[n]
                        if n < 2:
                            wge(tensor, "sa", a0)
                            wge(tensor, "sv", v0)
                            wge(tensor, "spv", pv0)
                            wge(tensor, "sev", cnt("sev"))
                            if qc == 0 and kv == 0 and n == 0:
                                wev(tensor, ("qkvdone", av))
                        else:
                            wge(tensor, "sa", a0 + n - 1)
                        bank = ps_s[n % 2]
                        qcol = (kv * H + qh) * TBATCH + qc * 512
                        ins = tensor.matmul(
                            bank[:, :],
                            K_fm[:, kv * TBATCH + kb * P:kv * TBATCH + (kb + 1) * P],
                            Q_fm[:, qcol:qcol + 512],
                            start=True, stop=not causal)
                        if causal:
                            ins = tensor.matmul(
                                bank[:, :],
                                alibi_kk[0:6, kb * P:(kb + 1) * P],
                                alibi_qq[0:6, qc * 512:(qc + 1) * 512],
                                start=False, stop=True)
                        inc(ins, "stn")
                        if qh == 1 and kb >= 1:
                            pvbatch(kb - 1)
                    pvbatch(kbc - 1)
                    for qh in range(H):
                        wge(tensor, "sv", v0 + 2 * qh + 1)
                        if qh >= 1:
                            wge(tensor, "sv", v0 + 2 * qh)
                        ins = tensor.matmul(ps_s[0][:, :], ones_rf[0:1, 0:P],
                                            ascr['r'][0:1, :], start=True, stop=True)
                        inc(ins, "stn")
                blk.tensor(f_t)

                def f_sc(sc, qc=qc, kv=kv, N=N, nfo=nfo, t0=t0, a0=a0, v0=v0,
                         pv0=pv0, prv0=prv0, cdiag=cdiag, kbc=kbc):
                    PB = 4 // H
                    for n in range(N):
                        kb, qh, diag = nfo[n]
                        bi = (kb % PB) * H + qh
                        wge(sc, "stn", t0 + n + 1)
                        wge(sc, "spv", pv0 + max(0, kb - PB + 1))
                        bank = ps_s[n % 2]
                        if diag:
                            wge(sc, "sprv", prv0 + cdiag[n] + 1)
                            inc(sc.activation(ascr['ptil'][bi][:, :], ascr['sm'][:, :],
                                              AF.Exp, bias=cbias[:, 1:2]),
                                "sa")
                        else:
                            inc(sc.activation(ascr['ptil'][bi][:, :], bank[:, :],
                                              AF.Exp, bias=cbias[:, 1:2]), "sa")
                    for qh in range(H):
                        wge(sc, "spv", pv0 + kbc)
                        wge(sc, "sv", v0 + 2 * qh)
                        inc(sc.activation(ascr['oev'][:, :], ps_o[kv * H + qh][:, :],
                                          AF.Copy), "sa")
                blk.scalar(f_sc)

                def f_v(v, qc=qc, kv=kv, N=N, nfo=nfo, t0=t0, a0=a0, v0=v0,
                        pv0=pv0, prv0=prv0, cdiag=cdiag, kbc=kbc, w0=w0,
                        swp0=swp0, prevdiag=prevdiag):
                    first_mask = True
                    for n in range(N):
                        kb, qh, diag = nfo[n]
                        if not diag:
                            continue
                        if first_mask and qc == 0 and kv == 0:
                            wev(v, "amask")
                        first_mask = False
                        wge(v, "stn", t0 + n + 1)
                        if cdiag[n] > 0:
                            wge(v, "sa", a0 + prevdiag[n] + 1)
                        di = kb - 4 * qc
                        inc(v.tensor_add(ascr['sm'][:, :], ps_s[n % 2][:, :],
                                         masks[:, di * 512:(di + 1) * 512]),
                            "sprv")
                    for qh in range(H):
                        wge(v, "spv", pv0 + kbc)
                        wge(v, "stn", t0 + N + qh)
                        inc(v.reciprocal(ascr['r'][0:1, :],
                                         ps_r[32 * qh:32 * qh + 1, :]), "sv")
                        wge(v, "sa", a0 + N + qh + 1)
                        wge(v, "stn", t0 + N + qh + 1)
                        hl = kv * H + qh
                        hlp = hl - 2
                        if hlp >= 0:
                            wge(v, f"swp{hlp}", swp0[hlp] + 16 * (qc + 1))
                        elif qc >= 1:
                            wge(v, f"swp{hl + 2}", swp0[hl + 2] + 16 * qc)
                        inc(v.tensor_mul(ascr['ost'][qh % 2][:, :], ascr['oev'][:, :],
                                         ps_s[0][:, :]), "sv")
                blk.vector(f_v)

                def f_sy(sync, qc=qc, kv=kv, H=H, v0=v0, o_cdram=o_cdram):
                    for qh in range(H):
                        wge(sync, "sv", v0 + 2 * qh + 2)
                        hl = kv * H + qh
                        inc(sync.dma_start(
                            o_cdram[qc * 512 + hl * P:qc * 512 + (hl + 1) * P,
                                    :],
                            ascr['ost'][qh % 2][:, :]), f"swp{hl}", 16)
                blk.sync(f_sy)
            def f_gp(gp, qc=qc, swp0=swp0):
                for i in range(4):
                    wge(gp, f"swp{i}", swp0[i] + 16 * (qc + 1))
                inc(gp.collective_compute(
                    "AllGather", ALU.bypass,
                    replica_groups=[[0, 1, 2, 3], [4, 5, 6, 7]],
                    ins=[bass.AP(o_cdram.tensor, qc * 512 * 512,
                                 [[512, 512], [1, 512]])],
                    outs=[bass.AP(o_alld_cur.tensor, qc * D * 512,
                                  [[512, 4 * 512], [1, 512]])]), "scc")
                ev[("ag_o", av, qc)] = ("scc", cnt("scc"))
            o_alld_cur = o_alld[av]
            o_cdram = o_cd[av]
            blk.gpsimd(f_gp)

    def wo_phase(av, O_own, wo_sbuf, wo_w, lx):
        t0, e0 = cnt("stn"), cnt("sev")
        sv_entry = cnt("sv")
        gw0 = [cnt("sgw0"), cnt("sgw1")]

        def f_gp(gp):
            with gp.register(f"ragw{av}") as ragw:
                gp.reg_load(ragw, agw_sb[0:1, av:av + 1])
                gp.wait_ge(sem("scc"), ragw)
            g0os = [cnt(f"sgo{k}") for k in range(4)]
            for ti in range(32):
                k = ti % 4
                if ti >= 4:
                    wge(gp, f"sgo{k}", g0os[k] + 16 * (ti // 4))
                inc(gp.indirect_dma_start(
                    O_own[:, ti * 512:(ti + 1) * 512], None,
                    o_alld[av][:, :],
                    bass.IndirectOffsetOnAxis(oaidx_sb[:, ti:ti + 1], 0)),
                    f"sgo{k}", 16)
            ev[("oga", av)] = [(f"sgo{k}", cnt(f"sgo{k}")) for k in range(4)]
            for nd in range(4):
                if nd >= 2:
                    wge(gp, "stn", t0 + (nd - 1) * 8)
                src = bass.AP(wo_w.tensor, nd * 512,
                              [[D, P], [P * D, NCH], [1, 512]])
                inc(gp.dma_start(wo_sbuf[nd % 2][:, :], src),
                    f"sgw{nd % 2}", 16)
                ev[("wosl", av, nd)] = (f"sgw{nd % 2}",
                                        cnt(f"sgw{nd % 2}"))
        blk.gpsimd(f_gp)

        def f_t(tensor):
            wev(tensor, ("oga", av))
            wge(tensor, "sv", sv_entry)
            wge(tensor, "sev", e0)
            for nd in range(4):
                wev(tensor, ("wosl", av, nd))
                for tt in range(NTILE):
                    gi = nd * 8 + tt
                    if gi >= 2:
                        wge(tensor, "sev", e0 + gi - 1)
                    ins = None
                    for hc in range(NCH):
                        ins = tensor.matmul(
                            ps_s[gi % 2][:, :],
                            O_own[:, ((tt // 4) * 16 + hc) * 512 +
                                  (tt % 4) * P:
                                  ((tt // 4) * 16 + hc) * 512 + (tt % 4 + 1) * P],
                            wo_sbuf[nd % 2][:, hc * 512:(hc + 1) * 512],
                            start=(hc == 0), stop=(hc == NCH - 1))
                    inc(ins, "stn")
        blk.tensor(f_t)

        def f_v(v):
            for nd in range(4):
                for tt in range(NTILE):
                    gi = nd * 8 + tt
                    wge(v, "stn", t0 + gi + 1)
                    inc(v.scalar_tensor_tensor(
                        x_sb[:, tt * D + nd * 512:tt * D + (nd + 1) * 512],
                        ps_s[gi % 2][:, :], 1.0,
                        x_sb[:, tt * D + nd * 512:tt * D + (nd + 1) * 512],
                        ALU.mult, ALU.add), "sev")
                    if nd == 3:
                        ev[("xf", lx, tt)] = ("sev", cnt("sev"))
        blk.vector(f_v)
        ev[("wodone", av)] = ("stn", cnt("stn"))

    def gating_phase(fi, lnf):
        for tt in range(NTILE):
            def f_t1(tensor, tt=tt):
                if tt == 0:
                    wev(tensor, ("lnslab", fi))
                    wev(tensor, "g32")
                else:
                    wev(tensor, ("grk", fi, tt - 1))
                ins = None
                for cc in range(NCH):
                    ins = tensor.matmul(
                        ps_r[:, 0:32],
                        lnf[:, cc * TCORE + tt * P:cc * TCORE + tt * P + P],
                        gwsb[:, cc * 32:(cc + 1) * 32],
                        start=(cc == 0), stop=(cc == NCH - 1))
                inc(ins, "stn")
                ev[("glog", fi, tt)] = ("stn", cnt("stn"))
            blk.tensor(f_t1)

            def f_v1(v, tt=tt):
                wev(v, ("glog", fi, tt))
                inc(v.reduce_max(gs2[:, 0:1], ps_r[:, 0:32], axis=AX.X), "sv")
                wge(v, "sv", cnt("sv"))
                inc(v.tensor_scalar(gs3[:, 0:1], gs2[:, 0:1], -1.0, None,
                                    ALU.mult), "sv")
                ev[("gnm", fi, tt)] = ("sv", cnt("sv"))
            blk.vector(f_v1)

            def f_s1(sc, tt=tt):
                wev(sc, ("gnm", fi, tt))
                inc(sc.activation(probs[:, :], ps_r[:, 0:32], AF.Exp,
                                  bias=gs3[:, 0:1], accum_out=gs1[:, 0:1]),
                    "sa")
                ev[("gexp", fi, tt)] = ("sa", cnt("sa"))
            blk.scalar(f_s1)

            def f_v2(v, tt=tt):
                wev(v, ("gexp", fi, tt))
                inc(v.reciprocal(gs4[:, 0:1], gs1[:, 0:1]), "sv")
                wge(v, "sv", cnt("sv"))
                inc(v.tensor_scalar(probs[:, :], probs[:, :], gs4[:, 0:1],
                                    None, ALU.mult), "sv")
                wge(v, "sv", cnt("sv"))
                in_j = bass.AP(probs, 0, [[32, P], [1, 32], [0, 32]])
                in_e = bass.AP(probs, 0, [[32, P], [0, 32], [1, 32]])
                o_c3 = bass.AP(cmp_sc, 0, [[1024, P], [32, 32], [1, 32]])
                inc(v.tensor_tensor(o_c3, in_j, in_e, ALU.is_gt), "sv")
                wge(v, "sv", cnt("sv"))
                in_r = bass.AP(cmp_sc, 0, [[1024, P], [1, 32], [32, 32]])
                o_r = bass.AP(rank_sb, 0, [[32, P], [1, 32], [0, 1]])
                inc(v.reduce_sum(o_r, in_r, axis=AX.X), "sv")
                wge(v, "sv", cnt("sv"))
                inc(v.tensor_single_scalar(sel_sb[:, :], rank_sb[:, :], 8.0,
                                           ALU.is_lt), "sv")
                wge(v, "sv", cnt("sv"))
                inc(v.tensor_mul(mp_sb[:, :], probs[:, :], sel_sb[:, :]), "sv")
                ev[("gmp", fi, tt)] = ("sv", cnt("sv"))
            blk.vector(f_v2)

            def f_t2(tensor, tt=tt):
                wev(tensor, ("gmp", fi, tt))
                if tt > 0:
                    wev(tensor, ("gwr2", fi, tt - 1))
                ins = tensor.transpose(ps_t[0:32, 0:P], mp_sb[:, 0:32],
                                       ident[:, :])
                inc(ins, "stn")
                ev[("gmpt", fi, tt)] = ("stn", cnt("stn"))
            blk.tensor(f_t2)

            def f_v3(v, tt=tt):
                wev(v, ("gmpt", fi, tt))
                inc(v.tensor_copy(mp_fm[0:32, 0:P], ps_t[0:32, 0:P]), "sv")
                ev[("gmpf", fi, tt)] = ("sv", cnt("sv"))
            blk.vector(f_v3)

            def f_t3(tensor, tt=tt):
                wev(tensor, ("gmpf", fi, tt))
                wev(tensor, ("glog", fi, tt))
                wev(tensor, ("gexp", fi, tt))
                ins = tensor.matmul(ps_r[:, 0:8], mp_fm[0:32, 0:P],
                                    G32[0:32, 0:8], start=True, stop=True)
                inc(ins, "stn")
                ev[("gwrm", fi, tt)] = ("stn", cnt("stn"))
            blk.tensor(f_t3)

            def f_v4(v, tt=tt):
                wev(v, ("gwrm", fi, tt))
                inc(v.tensor_copy(wraw[:, :], ps_r[:, 0:8]), "sv")
                ev[("grk", fi, tt)] = ("sv", cnt("sv"))
                wge(v, "sv", cnt("sv"))
                inc(v.reduce_sum(gs1[:, 0:1], wraw[:, :], axis=AX.X), "sv")
                wge(v, "sv", cnt("sv"))
                inc(v.reciprocal(gs4[:, 0:1], gs1[:, 0:1]), "sv")
                wge(v, "sv", cnt("sv"))
                inc(v.tensor_scalar(wtg_all[:, tt * 8:(tt + 1) * 8],
                                    wraw[:, :], gs4[:, 0:1], None, ALU.mult),
                    "sv")
                ev[("gwr2", fi, tt)] = ("sv", cnt("sv"))
            blk.vector(f_v4)
        ev[("gdone", fi)] = ("sv", cnt("sv"))

    def ffn_phase(fi, lnf, act_fm, dslab, gu_tl, fsc, lx, gate_loads):
        w_half, wbc, upsc, silu1 = fsc
        for h in range(2):
            t0 = cnt("stn")
            def fmm_val(ld, t0=t0):
                return t0 + (ld // 12) * 14 + 2 + (ld % 12) + 1

            def f_sy_gu(sync, h=h, t0=t0):
                ld = 0
                for g in range(8):
                    for ic in range(6):
                        for up in range(2):
                            oc = ic + 6 * up
                            if ld == 0:
                                wev(sync, "ag_gu")
                                if gate_loads is not None:
                                    wge(sync, gate_loads[0], gate_loads[1])
                            if ld >= 2:
                                wge(sync, "stn", fmm_val(ld - 2))
                            if ld < 2 and h == 1:
                                wge(sync, "stn", t0)
                            src = bass.AP(gu_all.tensor,
                                          g * D * 1536 + oc * P,
                                          [[1536, P], [P * 1536, NCH],
                                           [1, P]])
                            inc(sync.dma_start(gu_tl[ld % 2][:, :], src),
                                f"sf{ld % 2}", 16)
                            ev[("gusl", fi, h, ld)] = (f"sf{ld % 2}",
                                                       cnt(f"sf{ld % 2}"))
                            ld += 1
            blk.sync(f_sy_gu)

            e0, a0, v0 = cnt("sev"), cnt("sa"), cnt("sv")

            def f_t_gu(tensor, h=h, t0=t0, e0=e0, a0=a0, v0=v0):
                ld = 0
                for g in range(8):
                    if g == 0:
                        wev(tensor, ("gwr2", fi, h * 4 + 3))
                    if h == 0 and g == 0:
                        wev(tensor, ("lnslab", fi))
                    wge(tensor, "sv", v0 + g * 3)
                    ins = None
                    for t2 in range(4):
                        col = (h * 4 + t2) * 8 + g
                        ins = tensor.transpose(
                            bass.AP(ps_r, t2 * P, [[512, 1], [1, P]]),
                            wtg_all[:, col:col + 1], identf[:, :])
                    inc(ins, "stn")
                    wge(tensor, "sv", v0 + g * 3 + 1)
                    ins = tensor.matmul(ps_o[1][:, :], ones_rf[0:1, 0:P],
                                        w_half[0:1, :], start=True, stop=True)
                    inc(ins, "stn")
                    for ic in range(6):
                        for up in range(2):
                            wev(tensor, ("gusl", fi, h, ld))
                            i_g = g * 6 + ic
                            if up == 0 and i_g >= 1:
                                wge(tensor, "sa", a0 + i_g)
                            if up == 1 and i_g >= 1:
                                wge(tensor, "sev", e0 + (i_g - 1) * 2 + 1)
                            ins = None
                            for cc in range(NCH):
                                ins = tensor.matmul(
                                    ps_s[up][:, :],
                                    gu_tl[ld % 2][:, cc * P:(cc + 1) * P],
                                    lnf[:, cc * TCORE + h * 512:
                                        cc * TCORE + (h + 1) * 512],
                                    start=(cc == 0), stop=(cc == NCH - 1))
                            inc(ins, "stn")
                            ld += 1
            blk.tensor(f_t_gu)



            def f_s_gu(sc, h=h, t0=t0, e0=e0):
                for g in range(8):
                    for ic in range(6):
                        i_g = g * 6 + ic
                        wge(sc, "stn", t0 + g * 14 + 2 + 2 * ic + 1)
                        if i_g >= 1:
                            wge(sc, "sev", e0 + (i_g - 1) * 2 + 2)
                        inc(sc.activation(silu1[:, :], ps_s[0][:, :],
                                          AF.Silu), "sa")
            blk.scalar(f_s_gu)

            def f_v_gu(v, h=h, t0=t0, a0=a0, v0=v0, e0=e0):
                for g in range(8):
                    wge(v, "stn", t0 + g * 14 + 1)
                    inc(v.tensor_copy(w_half[0:1, :],
                                      bass.AP(ps_r, 0,
                                              [[512, 1], [1, 512]])), "sv")
                    wge(v, "stn", t0 + g * 14 + 2)
                    inc(v.tensor_copy(wbc[:, :], ps_o[1][:, :]), "sv")
                    inc(v.engine_nop(), "sv")
                    wge(v, "sv", v0 + g * 3 + 2)
                    for ic in range(6):
                        i_g = g * 6 + ic
                        wge(v, "stn", t0 + g * 14 + 2 + 2 * ic + 2)
                        inc(v.tensor_mul(upsc[:, :], ps_s[1][:, :],
                                         wbc[:, :]), "sev")
                        wge(v, "sev", e0 + i_g * 2 + 1)
                        wge(v, "sa", a0 + i_g + 1)
                        inc(v.tensor_mul(
                            act_fm[:, (g * 6 + ic) * 512:
                                   (g * 6 + ic + 1) * 512],
                            silu1[:, :], upsc[:, :]), "sev")
            blk.vector(f_v_gu)

            t0d = cnt("stn") + 16 * 14 - t0 + t0
            t0d = t0 + 8 * 14
            acts_done = e0 + 96

            def f_sy_dn(sync, h=h, t0d=t0d, fi=fi):
                for hc in range(8):
                    if hc >= 2:
                        wge(sync, "stn", t0d + (hc - 1) * 4)
                    if hc < 2:
                        wge(sync, "stn", t0d)
                        wev(sync, "ag_dn")
                    if hc < 2 and gate_loads is not None:
                        wge(sync, gate_loads[0], gate_loads[1])
                    src = bass.AP(dn_all.tensor, hc * 256 + h * 0,
                                  [[D, P], [P * D, 48], [1, 256]])
                    inc(sync.dma_start(dslab[hc % 2][:, :], src),
                        f"sd{hc % 2}", 16)
                    ev[("dnsl", fi, h, hc)] = (f"sd{hc % 2}",
                                               cnt(f"sd{hc % 2}"))
            blk.sync(f_sy_dn)

            e0d = cnt("sev")

            def f_t_dn(tensor, h=h, t0d=t0d, e0d=e0d, acts_done=acts_done):
                for hc in range(8):
                    wev(tensor, ("dnsl", fi, h, hc))
                    if hc == 0:
                        wge(tensor, "sev", acts_done)
                    for t2 in range(4):
                        gi = hc * 4 + t2
                        if gi >= 2:
                            wge(tensor, "sev", e0d + gi - 1)
                        ins = None
                        for g in range(8):
                            for ic in range(6):
                                k = g * 6 + ic
                                ins = tensor.matmul(
                                    bass.AP(ps_s[gi % 2], 0,
                                            [[512, P], [1, 256]]),
                                    act_fm[:, k * 512 + t2 * P:
                                           k * 512 + (t2 + 1) * P],
                                    dslab[hc % 2][:, k * 256:(k + 1) * 256],
                                    start=(k == 0), stop=(k == 47))
                        inc(ins, "stn")
            blk.tensor(f_t_dn)

            def f_v_dn(v, h=h, t0d=t0d, lx=lx):
                for hc in range(8):
                    for t2 in range(4):
                        gi = hc * 4 + t2
                        wge(v, "stn", t0d + gi + 1)
                        tt = h * 4 + t2
                        xo = x_sb[:, tt * D + hc * 256:
                                  tt * D + (hc + 1) * 256]
                        inc(v.scalar_tensor_tensor(
                            xo, bass.AP(ps_s[gi % 2], 0,
                                        [[512, P], [1, 256]]),
                            1.0, xo, ALU.mult, ALU.add), "sev")
                        if hc == 7:
                            ev[("xf", lx, tt)] = ("sev", cnt("sev"))
            blk.vector(f_v_dn)
        ev[("ffndone", fi)] = ("stn", cnt("stn"))

    sW0 = ExitStack()
    wqkvA = sbuf(sW0, "wqkvA", [P, QO + 16 * 256], BF16)
    sGth = ExitStack()
    gath32 = [sbuf(sGth, f"gath32_{i}", [P, D], F32) for i in range(2)]
    gath16 = [sbuf(sGth, f"gath16_{i}", [P, D], BF16) for i in range(2)]
    idx2 = [sbuf(sGth, f"idx2_{i}", [P, 1], I32) for i in range(2)]

    lnS = ExitStack()
    scr = alloc_ln_scratch(lnS, "A")
    ln_emit(0, [ev[("xload", tt)] for tt in range(NTILE)], scr, ln_cd[0])
    lnS.close()

    def mk_gp_ag_ln(li, av):
        def gp_ag_ln(gp):
            for h in range(2):
                for s, v in ev[("lnw", li, h)]:
                    wge(gp, s, v)
                inc(gp.collective_compute(
                    "AllGather", ALU.bypass,
                    replica_groups=[[0, 1, 2, 3], [4, 5, 6, 7]],
                    ins=[bass.AP(ln_cd[av].tensor, h * 512 * D,
                                 [[D, 512], [1, D]])],
                    outs=[bass.AP(ln_alld[av].tensor, h * 2048 * D,
                                  [[D, 2048], [1, D]])]), "scc")
                ev[("ag_ln", av, h)] = ("scc", cnt("scc"))
        return gp_ag_ln

    load_wqkv(0, wqkvA, 1, wq_a, wk_a, wv_a, None)
    blk.gpsimd(mk_gp_ag_ln(0, 0))

    def gp_dom(gp):
        wev(gp, "consts")
        inc(gp.indirect_dma_start(
            dom_sb[:, 0:1], None, dome_in[:, :],
            bass.IndirectOffsetOnAxis(cvec_sb[:, 0:1], 0)), "sgd", 16)
        ev["dom"] = ("sgd", cnt("sgd"))
        inc(gp.iota(iota_p[:, 0:1], [[0, 1]], channel_multiplier=1), "sio", 1)
        ev["iota"] = ("sio", cnt("sio"))
    blk.gpsimd(gp_dom)

    chunks = [("gu", i) for i in range(12)] + [("dn", i) for i in range(16)]
    v0g, t0g, e0g, w0g = (cnt("sv"), cnt("stn"), cnt("sev"), cnt("swc"))
    gg0 = [cnt("sgg0"), cnt("sgg1")]
    ngroups = []
    for i, (kind, ccc) in enumerate(chunks):
        nblk = 16 if kind == "gu" else 6
        for g in range((nblk + 7) // 8):
            ngroups.append((i, g, kind, ccc, min(8, nblk - g * 8)))
    cumg = [0]
    for kind, ccc in chunks:
        cumg.append(cumg[-1] + (2 if kind == "gu" else 1))
    grp_of_chunk = [[] for _ in chunks]
    for n, (i, g, kind, ccc, qn) in enumerate(ngroups):
        grp_of_chunk[i].append((n, qn))

    def gev(v, n, qn):
        wge(v, "stn", t0g + n + 1)
        if n >= 1:
            wge(v, "swc", w0g + n * 16)
        i_ap = bass.AP(ps_t, 0, [[1024, P], [P, qn], [1, P]])
        o_ap = bass.AP(cmp_sc, 0, [[1024, P], [P, qn], [1, P]])
        inc(v.tensor_copy(o_ap, i_ap), "sev")

    def f_v_idx(v):
        for i, (kind, ccc) in enumerate(chunks):
            if i == 0:
                wev(v, "iota")
                wev(v, "dom")
            rows = 1536 if kind == "gu" else D
            wid = D if kind == "gu" else 768
            if i >= 2:
                wge(v, f"sgg{i % 2}", gg0[i % 2] + 16 * (i // 2))
            inc(v.tensor_scalar(idx2[i % 2][:, 0:1], dom_sb[:, 0:1], rows,
                                ccc * P, ALU.mult, ALU.add), "sv")
            wge(v, "sv", v0g + 3 * i + 1)
            inc(v.tensor_add(idx2[i % 2][:, 0:1], idx2[i % 2][:, 0:1],
                             iota_p[:, 0:1]), "sv")
            wge(v, f"sgg{i % 2}",
                gg0[i % 2] + 16 * (i // 2 + 1))
            if i >= 1:
                for n, qn in grp_of_chunk[i - 1]:
                    gev(v, n, qn)
                wge(v, "stn", t0g + cumg[i])
            inc(v.tensor_copy(gath16[i % 2][:, 0:wid],
                              gath32[i % 2][:, 0:wid]), "sv")
        for n, qn in grp_of_chunk[len(chunks) - 1]:
            gev(v, n, qn)
        ev["gath_done"] = ("sev", cnt("sev"))
    blk.vector(f_v_idx)

    def gout(gp, n, qn, kind, ccc, g):
        wge(gp, "sev", e0g + n + 1)
        dstd = guc if kind == "gu" else dnc
        wid = 1536 if kind == "gu" else D
        d_ap = bass.AP(dstd.tensor, (g * 8) * P * wid + ccc * P,
                       [[wid, P], [P * wid, qn], [1, P]])
        s_ap = bass.AP(cmp_sc, 0, [[1024, P], [P, qn], [1, P]])
        inc(gp.dma_start(d_ap, s_ap), "swc", 16)

    def f_gp_gather(gp):
        for i, (kind, ccc) in enumerate(chunks):
            wid = D if kind == "gu" else 768
            srcd = gu_in if kind == "gu" else dn_in
            wge(gp, "sv", v0g + 3 * i + 2)
            if i >= 2:
                wge(gp, "sv", v0g + 3 * (i - 1))
            inc(gp.indirect_dma_start(
                gath32[i % 2][:, 0:wid], None, srcd[:, :],
                bass.IndirectOffsetOnAxis(idx2[i % 2][:, 0:1], 0)),
                f"sgg{i % 2}", 16)
            if i >= 1:
                kp, cp = chunks[i - 1]
                for (n, qn) in grp_of_chunk[i - 1]:
                    gg = n - grp_of_chunk[i - 1][0][0]
                    gout(gp, n, qn, kp, cp, gg)
        kp, cp = chunks[-1]
        for (n, qn) in grp_of_chunk[len(chunks) - 1]:
            gg = n - grp_of_chunk[len(chunks) - 1][0][0]
            gout(gp, n, qn, kp, cp, gg)
        ev["gwr"] = ("swc", cnt("swc"))
    blk.gpsimd(f_gp_gather)

    def f_t_g(tensor):
        for n, (i, g, kind, ccc, qn) in enumerate(ngroups):
            wge(tensor, "sv", v0g + 3 * i + 3)
            if n >= 1:
                wge(tensor, "sev", e0g + n)
            ins = None
            for q in range(qn):
                cc = g * 8 + q
                ins = tensor.transpose(ps_t[:, q * P:(q + 1) * P],
                                       gath16[i % 2][:, cc * P:(cc + 1) * P],
                                       ident[:, :])
            inc(ins, "stn")
    blk.tensor(f_t_g)


    def gp_ag_w(gp):
        wev(gp, "gwr")
        inc(gp.collective_compute(
            "AllGather", ALU.bypass, replica_groups=[list(range(8))],
            ins=[guc.opt()], outs=[gu_all.opt()]), "scc")
        ev["ag_gu"] = ("scc", cnt("scc"))
        inc(gp.collective_compute(
            "AllGather", ALU.bypass, replica_groups=[list(range(8))],
            ins=[dnc.opt()], outs=[dn_all.opt()]), "scc")
        ev["ag_dn"] = ("scc", cnt("scc"))
    blk.gpsimd(gp_ag_w)

    sGth.close()

    sA = ExitStack()
    alibi_kk = sbuf(sA, "alibi_kk", [6, TBATCH], BF16)
    alibi_qq = sbuf(sA, "alibi_qq", [6, TBATCH], BF16)
    masks = sbuf(sA, "masks", [P, 4 * 512], BF16)


    ascr['ptil'] = [sbuf(sA, f"ptilA{i}", [P, 512], BF16) for i in range(4)]
    ascr['sm'] = sbuf(sA, "smA", [P, 512], F32)
    ascr['oev'] = sbuf(sA, "oevA", [P, 512], BF16)
    ascr['r'] = sbuf(sA, "rA", [1, 512], F32)
    ascr['ost'] = [sbuf(sA, f"ostA{i}", [P, 512], BF16) for i in range(2)]
    Q_fmA = sbuf(sA, "Q_fmA", [P, 4 * TBATCH], BF16)
    K_fmA = sbuf(sA, "K_fmA", [P, TBATCH], BF16)
    V_tmA = sbuf(sA, "V_tmA", [P, TBATCH], BF16)
    sA3 = ExitStack()
    lslabA = [sbuf(sA3, f"lslabA{i}", [P, 16 * 512], BF16) for i in range(2)]
    qkv_phase(0, wqkvA, lslabA, Q_fmA, K_fmA, V_tmA, ln_alld[0], 1, None)
    sA3.close()
    def ld_amask(sync):
        wev(sync, "gath_done")
        inc(sync.dma_start(masks[:, :], masks_in[:, :]), "sxm", 16)
        inc(sync.dma_start(alibi_kk[0:6, :], alibi_in[0:6, :]), "sxm", 16)
        inc(sync.dma_start(alibi_qq[0:6, :], alibi_in[6:12, :]), "sxm", 16)
        ev["amask"] = ("sxm", cnt("sxm"))
    blk.sync(ld_amask)

    def gate_attn_amask(tensor):
        wev(tensor, "amask")
    blk.tensor(gate_attn_amask)

    attn_phase(0, Q_fmA, K_fmA, V_tmA, 1, None, o_cd[0])
    _SW0_CLOSE_MARK = True
    if DBG:
        def f_dbgq(sync):
            s, v = ev[("qkvdone", 0)]
            wge(sync, s, v)
            inc(sync.dma_start(dbg_qfm[:, :], Q_fmA[:, :]), "sw", 16)
            inc(sync.dma_start(dbg_kfm[:, :], K_fmA[:, :]), "sw", 16)
            inc(sync.dma_start(dbg_vtm[:, :], V_tmA[:, :]), "sw", 16)
        blk.sync(f_dbgq)
    sA.close()

    sW = ExitStack()
    O_ownA = sbuf(sW, "O_ownA", [P, 32 * 512], BF16)
    wo_sbA = [sbuf(sW, f"wo_sbA{i}", [P, 16 * 512], BF16) for i in range(2)]
    wo_phase(0, O_ownA, wo_sbA, wo_a, 1)
    if DBG:
        def f_dbgo(sync):
            wge(sync, "stn", cnt("stn"))
            inc(sync.dma_start(dbg_oown[:, :], O_ownA[:, :]), "sw", 16)
            inc(sync.dma_start(dbg_wosb[:, :], wo_sbA[1][:, :]), "sw", 16)
        blk.sync(f_dbgo)
    sW.close()
    sW0.close()

    if UPTO >= 2:
        sF = ExitStack()
        ln2_fm = sbuf(sF, "ln2_fm", [P, NCH * TCORE], BF16)
        sL2 = ExitStack()
        scr2 = alloc_ln_scratch(sL2, "B")
        ln_to_slab(0, 10, [ev[("xf", 1, tt)] for tt in range(NTILE)],
                   scr2, ln2_fm, gate_first=ev[("wodone", 0)])
        sL2.close()
        gating_phase(0, ln2_fm)
        sFC = ExitStack()
        act_fm1 = sbuf(sFC, "act_fm1", [P, 48 * 512], BF16)
        sh48a = sbuf(sFC, "sh48a", [P, 2 * 48 * 256], BF16)
        dslab1 = [sh48a[:, i * 48 * 256:(i + 1) * 48 * 256] for i in range(2)]
        gu_tl1 = [sh48a[:, i * 2048:(i + 1) * 2048] for i in range(2)]
        fsc1 = (sbuf(sFC, "w_half1", [1, 512], F32),
                sbuf(sFC, "wbc1", [P, 512], F32),
                sbuf(sFC, "upsc1", [P, 512], BF16),
                sbuf(sFC, "silu1a", [P, 512], BF16))
        ffn_phase(0, ln2_fm, act_fm1, dslab1, gu_tl1, fsc1, 2,
                  ev[("wodone", 0)])
        sFC.close()
        sF.close()

    if UPTO >= 3:
        sL3 = ExitStack()
        scr3 = alloc_ln_scratch(sL3, "C")
        ln_emit(1, [ev[("xf", 2, tt)] for tt in range(NTILE)], scr3,
                ln_cd[1], gate_first=ev[("ffndone", 0)])
        sL3.close()
        blk.gpsimd(mk_gp_ag_ln(1, 1))

        sB = ExitStack()
        ascr['ptil'] = [sbuf(sB, f"ptilB{i}", [P, 512], BF16)
                        for i in range(4)]
        ascr['oev'] = sbuf(sB, "oevB", [P, 512], BF16)
        ascr['r'] = sbuf(sB, "rB", [1, 512], F32)
        ascr['ost'] = [sbuf(sB, f"ostB{i}", [P, 512], BF16) for i in range(2)]
        Q_fmB = sbuf(sB, "Q_fmB", [P, 4 * TBATCH], BF16)
        K_fmB = sbuf(sB, "K_fmB", [P, 2 * TBATCH], BF16)
        V_tmB = sbuf(sB, "V_tmB", [P, 2 * TBATCH], BF16)
        sB3 = ExitStack()
        wqkvB = sbuf(sB3, "wqkvB", [P, QO + 16 * 512], BF16)
        lslabB = [sbuf(sB3, f"lslabB{i}", [P, 16 * 512], BF16)
                  for i in range(2)]
        load_wqkv(1, wqkvB, 2, wq_b, wk_b, wv_b, ev[("ffndone", 0)])
        qkv_phase(1, wqkvB, lslabB, Q_fmB, K_fmB, V_tmB, ln_alld[1], 2,
                  ev[("ffndone", 0)])
        sB3.close()
        attn_phase(1, Q_fmB, K_fmB, V_tmB, 2, None, o_cd[1])
        sB.close()

        sW2 = ExitStack()
        O_ownB = sbuf(sW2, "O_ownB", [P, 32 * 512], BF16)
        wo_sbB = [sbuf(sW2, f"wo_sbB{i}", [P, 16 * 512], BF16)
                  for i in range(2)]
        wo_phase(1, O_ownB, wo_sbB, wo_b, 3)
        sW2.close()

    if UPTO >= 4:
        sF2 = ExitStack()
        ln4_fm = sbuf(sF2, "ln4_fm", [P, NCH * TCORE], BF16)
        sL4 = ExitStack()
        scr4 = alloc_ln_scratch(sL4, "Dd")
        ln_to_slab(1, 11, [ev[("xf", 3, tt)] for tt in range(NTILE)],
                   scr4, ln4_fm, gate_first=ev[("wodone", 1)])
        sL4.close()
        gating_phase(1, ln4_fm)
        sFC2 = ExitStack()
        act_fm2 = sbuf(sFC2, "act_fm2", [P, 48 * 512], BF16)
        sh48b = sbuf(sFC2, "sh48b", [P, 2 * 48 * 256], BF16)
        dslab2 = [sh48b[:, i * 48 * 256:(i + 1) * 48 * 256] for i in range(2)]
        gu_tl2 = [sh48b[:, i * 2048:(i + 1) * 2048] for i in range(2)]
        fsc2 = (sbuf(sFC2, "w_half2", [1, 512], F32),
                sbuf(sFC2, "wbc2", [P, 512], F32),
                sbuf(sFC2, "upsc2", [P, 512], BF16),
                sbuf(sFC2, "silu2a", [P, 512], BF16))
        ffn_phase(1, ln4_fm, act_fm2, dslab2, gu_tl2, fsc2, 4,
                  ev[("wodone", 1)])
        sFC2.close()
        sF2.close()

    out_lx = UPTO

    def f_out(sync):
        for tt in range(NTILE):
            s, vv = ev[("xf", out_lx, tt)]
            wge(sync, s, vv)
            inc(sync.dma_start(x_out[tt * P:(tt + 1) * P, :],
                               x_sb[:, tt * D:(tt + 1) * D]), "sw", 16)
        if DBG:
            inc(sync.dma_start(dbg_lnall[:, :], ln_alld[0][:, :]), "sw", 16)
            inc(sync.dma_start(dbg_oall[:, :], o_alld[0][:, :]), "sw", 16)
        wge(sync, "sw", cnt("sw"))
    blk.sync(f_out)

    top.close()
    _PROG_CACHE["p"] = nc
    return nc


def _split3(v):
    v = np.asarray(v, np.float32)
    hi = v.astype(BF)
    mid = (v - hi.astype(np.float32)).astype(BF)
    lo = (v - hi.astype(np.float32) - mid.astype(np.float32)).astype(BF)
    return hi, mid, lo


def _host_inputs(inputs, c):
    b, j = c // 4, c % 4
    x = inputs["L0_attn_a_x"]
    Wq, Wk, Wv, Wo = (inputs["L0_attn_a_Wq"], inputs["L0_attn_a_Wk"],
                      inputs["L0_attn_a_Wv"], inputs["L0_attn_a_Wo"])
    wqb, wkb, wvb, wob = (inputs["L0_attn_b_wq"], inputs["L0_attn_b_wk"],
                          inputs["L0_attn_b_wv"], inputs["L0_attn_b_wo"])
    gwe = inputs["L0_ffn_gate_weight"]
    gup = inputs["L0_ffn_gate_up_proj"]
    dwn = inputs["L0_ffn_down_proj"]
    mg = np.asarray(inputs["L0_ffn_merge_groups"], np.int32).reshape(32, 1)
    dome = np.asarray(inputs["L0_ffn_dominant_experts"],
                      np.int32).reshape(8, 1)
    C = np.ascontiguousarray
    d = {
        "x_in": C(x[b, j * 1024:(j + 1) * 1024, :]),
        "wq_a": C(Wq[:, 4 * j * 128:(4 * j + 4) * 128]),
        "wk_a": C(Wk[:, j * 128:(j + 1) * 128]),
        "wv_a": C(Wv[:, j * 128:(j + 1) * 128]),
        "wo_a": C(Wo),
        "wq_b": C(wqb[4 * j * 128:(4 * j + 4) * 128, :].T),
        "wk_b": C(wkb[2 * j * 128:(2 * j + 2) * 128, :].T),
        "wv_b": C(wvb[2 * j * 128:(2 * j + 2) * 128, :].T),
        "wo_b": C(wob.T),
        "gw_t": C(gwe.T),
        "gu_in": C(gup.reshape(32 * 1536, D)),
        "dn_in": C(dwn.reshape(32 * D, 768)),
        "mg_in": mg,
        "dome_in": dome,
        "cvec_in": np.full((P, 1), c, np.int32),
        "ident_in": np.eye(P, dtype=np.float32).astype(BF),
        "identf_in": np.eye(P, dtype=np.float32),
        "cbias_in": np.tile(np.array([[LN_EPS, -M_STAB]], np.float32),
                            (P, 1)),
        "agw_in": np.array([[6 + 2 * j, 16 + 2 * j]], np.int32),
    }
    m = np.zeros((P, 4 * 512), np.float32)
    ql = np.arange(512)
    for di in range(4):
        kl = np.arange(P)[:, None] + di * 128
        m[:, di * 512:(di + 1) * 512] = np.where(kl <= ql[None, :],
                                                 0.0, -1e30)
    d["masks_in"] = m.astype(BF)
    slope = SL16[j]
    al = np.zeros((12, TBATCH), np.float32)
    kq = np.arange(TBATCH, dtype=np.float64)
    h, mi, lo = _split3(slope * kq)
    al[0, :] = h.astype(np.float32)
    al[1, :] = mi.astype(np.float32)
    al[2, :] = lo.astype(np.float32)
    al[3:6, :] = 1.0
    al[6:9, :] = 1.0
    h, mi, lo = _split3(-slope * kq)
    al[9, :] = h.astype(np.float32)
    al[10, :] = mi.astype(np.float32)
    al[11, :] = lo.astype(np.float32)
    d["alibi_in"] = al.astype(BF)
    oi = np.zeros((P, 32), np.int32)
    pcol = np.arange(P, dtype=np.int32)
    ti = 0
    for k in range(2):
        qc = 2 * j + k
        for r in range(4):
            for hl in range(4):
                oi[:, ti] = qc * D + r * 512 + hl * 128 + pcol
                ti += 1
    d["oaidx_in"] = oi
    return d


def kernel(**inputs):
    nc = build_program()
    in_maps = [_host_inputs(inputs, c) for c in range(8)]
    res = run_bass_kernel_spmd(nc, in_maps, core_ids=list(range(8)))
    x = inputs["L0_attn_a_x"]
    out = np.zeros((2, TBATCH, D), np.float32)
    for c in range(8):
        b, j = c // 4, c % 4
        out[b, j * 1024:(j + 1) * 1024, :] = res.results[c]["x_out"]
    return out


if __name__ == "__main__":
    build_program()
    print("build ok")
